# version 02
# baseline (speedup 1.0000x reference)
import numpy as np
from contextlib import ExitStack

import concourse.bass as bass
import concourse.mybir as mybir
from concourse.bass_utils import run_bass_kernel_spmd

F32 = mybir.dt.float32
BF16 = mybir.dt.bfloat16
I8 = mybir.dt.int8
AF = mybir.ActivationFunctionType
ALU = mybir.AluOpType
AX = mybir.AxisListType

NCORE = 8
B, L, D = 2, 8192, 2048
CHUNK = 2048
NBLK, TB = 2, 1024
DT = 16
NST = 16
DTR = 64
INTER = 8192
MI = 64
HD_SCALE = float(128 ** -0.5)
INV_D = 1.0 / 2048.0
LN_EPS = 1e-5
A_EPS = 1e-9

ENGINES = ("tensor", "vector", "scalar", "gpsimd", "sync")


class Prog:

    OWN = {"tensor": "PE", "vector": "VE", "scalar": "SE"}

    def __init__(self, nc):
        self.nc = nc
        self.ops = {e: [] for e in ENGINES}
        self.counts = {}

    def tok(self, sem):
        return (sem, self.counts.get(sem, 0), 0)

    def emit(self, eng, fn, waits=(), inc=None, inc_by=1):
        tok = None
        if inc is not None:
            self.counts[inc] = self.counts.get(inc, 0) + inc_by
            tok = (inc, self.counts[inc], inc_by)
        ws = tuple((w[0], w[1]) for w in waits if w is not None)
        self.ops[eng].append((ws, fn, tok))
        return tok

    def barrier(self):
        toks = [(s, self.counts[s]) for s in ("PE", "VE", "SE", "V2", "WL", "XL", "ST")
                if s in self.counts]
        for e in ENGINES:
            self.ops[e].append((tuple(toks), None, None))

    def finalize(self, stack):
        nc = self.nc
        sems = {name: stack.enter_context(nc.semaphore(name))
                for name in self.counts}
        block = stack.enter_context(nc.Block())

        def mkbody(ops, own):
            def body(eng):
                waited = {}
                for ws, fn, tok in ops:
                    for (s, v) in ws:
                        if s == own or waited.get(s, 0) >= v:
                            continue
                        eng.wait_ge(sems[s], v)
                        waited[s] = v
                    inst = fn(eng) if fn is not None else None
                    if tok is not None and inst is not None:
                        inst.then_inc(sems[tok[0]], tok[2])
            return body

        for engname in ENGINES:
            getattr(block, engname)(mkbody(self.ops[engname], Prog.OWN.get(engname)))



def alloc_common(nc, stack, kind):
    bu = {}
    e = stack.enter_context
    bu["xblk"] = e(nc.sbuf_tensor("xblk", [128, DT * TB], F32))
    bu["ln_bf"] = e(nc.sbuf_tensor("ln_bf", [128, DT * TB], BF16))
    if kind != "l1":
        bu["ubuf"] = e(nc.sbuf_tensor("ubuf", [128, 16384], BF16))
    nw = 4096 if kind == "l1" else 8192
    bu["wsl"] = [e(nc.sbuf_tensor(f"wsl{i}", [128, nw], BF16)) for i in (0, 1)]
    bu["tsc"] = e(nc.sbuf_tensor("tsc", [128, 3072], F32))
    bu["tbf"] = e(nc.sbuf_tensor("tbf", [128, 5120], BF16))
    bu["cst"] = e(nc.sbuf_tensor("cst", [128, 8], F32))
    bu["cstb"] = e(nc.sbuf_tensor("cstb", [128, 2], BF16))
    bu["onesr"] = e(nc.sbuf_tensor("onesr", [1, 128], F32))
    bu["onesrb"] = e(nc.sbuf_tensor("onesrb", [1, 128], BF16))
    bu["rbc"] = e(nc.sbuf_tensor("rbc", [128, TB], BF16))
    bu["nmrbc"] = e(nc.sbuf_tensor("nmrbc", [128, TB], BF16))
    bu["onest"] = e(nc.sbuf_tensor("onest", [128, TB], BF16))
    bu["bank"] = [e(nc.psum_tensor(f"pb{i}", [128, 512], F32)) for i in range(8)]
    return bu


def emit_consts(p, bu):
    cst, cstb = bu["cst"], bu["cstb"]
    for i, v in enumerate([1.0, LN_EPS, -0.5, 30.0, A_EPS, 0.0, 0.0, 0.0]):
        p.emit("vector", lambda e, i=i, v=v: e.memset(cst[:, i:i + 1], v), inc="VE")
    p.emit("vector", lambda e: e.memset(cstb[:, 0:1], 1.0), inc="VE")
    p.emit("vector", lambda e: e.memset(bu["onesr"][:, :], 1.0), inc="VE")
    p.emit("vector", lambda e: e.memset(bu["onesrb"][:, :], 1.0), inc="VE")
    p.emit("vector", lambda e: e.memset(bu["onest"][:, :], 1.0), inc="VE")



def emit_ln(p, bu, xld_tok):
    xblk, ln_bf, tbf = bu["xblk"], bu["ln_bf"], bu["tbf"]
    cst, cstb, onesr = bu["cst"], bu["cstb"], bu["onesr"]
    rbc, nmrbc, bank = bu["rbc"], bu["nmrbc"], bu["bank"]
    srow = bu["tsc"]

    xbf = [tbf[:, 0:TB], tbf[:, TB:2 * TB]]
    sqb = [tbf[:, 2 * TB:3 * TB], tbf[:, 3 * TB:4 * TB]]

    mm_gate = [None, None]
    prev_sq = [None, None]
    for dt in range(DT):
        r = dt % 2
        cp = p.emit("vector",
                    lambda e, dt=dt, r=r: e.tensor_copy(
                        xbf[r][:, :], xblk[:, dt * TB:(dt + 1) * TB]),
                    waits=[xld_tok, mm_gate[r]], inc="VE")
        sq = p.emit("scalar",
                    lambda e, r=r: e.activation(sqb[r][:, :], xbf[r][:, :], AF.Square),
                    waits=[cp, mm_gate[r]], inc="SE")
        last = None
        for ch in range(2):
            p.emit("tensor",
                   lambda e, r=r, ch=ch, dt=dt: e.matmul(
                       bank[4 + ch][0:1, :], cstb[:, 0:1],
                       xbf[r][:, ch * 512:(ch + 1) * 512],
                       start=(dt == 0), stop=(dt == DT - 1)),
                   waits=[cp], inc="PE")
            last = p.emit("tensor",
                          lambda e, r=r, ch=ch, dt=dt: e.matmul(
                              bank[6 + ch][0:1, :], cstb[:, 0:1],
                              sqb[r][:, ch * 512:(ch + 1) * 512],
                              start=(dt == 0), stop=(dt == DT - 1)),
                          waits=[sq], inc="PE")
        mm_gate[r] = last
    stats_done = p.tok("PE")

    r_mean = srow[0:1, 0:512]
    r_rstd = srow[0:1, 512:1024]
    r_nmr = srow[0:1, 1024:1536]
    prev_bcast = None
    for ch in range(2):
        t0 = p.emit("vector", lambda e, ch=ch: e.tensor_scalar_mul(
            r_mean, bank[4 + ch][0:1, :], INV_D),
            waits=[stats_done, prev_bcast], inc="VE")
        p.emit("vector", lambda e: e.tensor_mul(r_nmr, r_mean, r_mean), inc="VE")
        t1 = p.emit("vector", lambda e, ch=ch: e.scalar_tensor_tensor(
            r_rstd, bank[6 + ch][0:1, :], INV_D, r_nmr, ALU.mult, ALU.subtract),
            inc="VE")
        s1 = p.emit("scalar", lambda e: e.activation(
            r_rstd, r_rstd, AF.Ln, bias=cst[0:1, 1:2]), waits=[t1], inc="SE")
        s2 = p.emit("scalar", lambda e: e.activation(
            r_rstd, r_rstd, AF.Exp, scale=cst[0:1, 2:3]), waits=[s1], inc="SE")
        nm = p.emit("vector", lambda e: e.scalar_tensor_tensor(
            r_nmr, r_mean, -1.0, r_rstd, ALU.mult, ALU.mult),
            waits=[s2], inc="VE")
        m1 = p.emit("tensor", lambda e, ch=ch: e.matmul(
            bank[4 + ch][:, :], onesr[0:1, :], r_rstd, start=True, stop=True),
            waits=[nm], inc="PE")
        m2 = p.emit("tensor", lambda e, ch=ch: e.matmul(
            bank[6 + ch][:, :], onesr[0:1, :], r_nmr, start=True, stop=True),
            waits=[nm], inc="PE")
        prev_bcast = m2
        ev1 = p.emit("vector", lambda e, ch=ch: e.tensor_copy(
            rbc[:, ch * 512:(ch + 1) * 512], bank[4 + ch][:, :]),
            waits=[m2], inc="VE")
        ev2 = p.emit("vector", lambda e, ch=ch: e.tensor_copy(
            nmrbc[:, ch * 512:(ch + 1) * 512], bank[6 + ch][:, :]),
            waits=[m2], inc="VE")
    bc_ev = p.tok("VE")

    for dt in range(DT):
        r = dt % 2
        p.emit("vector", lambda e, dt=dt, r=r: e.tensor_copy(
            xbf[r][:, :], xblk[:, dt * TB:(dt + 1) * TB]),
            waits=[bc_ev], inc="VE")
        p.emit("vector", lambda e, r=r: e.tensor_mul(
            sqb[r][:, :], xbf[r][:, :], rbc[:, :]), inc="VE")
        p.emit("vector", lambda e, dt=dt, r=r: e.tensor_add(
            ln_bf[:, dt * TB:(dt + 1) * TB], sqb[r][:, :], nmrbc[:, :]), inc="VE")
    return p.tok("VE")



def emit_ffn(p, bu, ins, ln_tok):
    xblk, ln_bf, ubuf = bu["xblk"], bu["ln_bf"], bu["ubuf"]
    wsl, tbf, bank = bu["wsl"], bu["tbf"], bu["bank"]
    sg_c, su_c, sd_c = bu["sg_c"], bu["su_c"], bu["sd_c"]

    g_bf = tbf[:, 0:256]
    u_bf = tbf[:, 256:512]

    prev_ft = None
    for fb in range(4):
        slab_free = [None, None]
        bank_free = [None, None, None, None]
        for g in range(32):
            r = g % 2
            sl = wsl[r]
            mi0 = 2 * g
            p.emit("gpsimd", lambda e, sl=sl, mi0=mi0: e.dma_start(
                out=sl[:, 0:4096].rearrange("p (kt m) -> p kt m", m=256),
                in_=ins["wg"][:, mi0 * 128:(mi0 + 2) * 128].rearrange(
                    "(kt p) m -> p kt m", p=128)),
                waits=[slab_free[r]], inc="WL", inc_by=16)
            sr = p.emit("gpsimd", lambda e, sl=sl, mi0=mi0: e.dma_start(
                out=sl[:, 4096:8192].rearrange("p (kt m) -> p kt m", m=256),
                in_=ins["wu"][:, mi0 * 128:(mi0 + 2) * 128].rearrange(
                    "(kt p) m -> p kt m", p=128)),
                inc="WL", inc_by=16)
            lastmm = None
            b0, b1 = 2 * r, 2 * r + 1
            for kt in range(DT):
                for sub in range(2):
                    bk = (b0, b1)[sub]
                    p.emit("tensor", lambda e, sl=sl, kt=kt, sub=sub, bk=bk, fb=fb: e.matmul(
                        bank[bk][:, 0:256],
                        sl[:, kt * 256 + sub * 128: kt * 256 + sub * 128 + 128],
                        ln_bf[:, kt * TB + fb * 256: kt * TB + fb * 256 + 256],
                        start=(kt == 0), stop=False),
                        waits=[sr, ln_tok, bank_free[bk]], inc="PE")
                    lastmm = p.emit("tensor", lambda e, sl=sl, kt=kt, sub=sub, bk=bk, fb=fb: e.matmul(
                        bank[bk][:, 256:512],
                        sl[:, 4096 + kt * 256 + sub * 128: 4096 + kt * 256 + sub * 128 + 128],
                        ln_bf[:, kt * TB + fb * 256: kt * TB + fb * 256 + 256],
                        start=False, stop=(kt == DT - 1)),
                        inc="PE")
            slab_free[r] = lastmm
            for sub in range(2):
                bk = (b0, b1)[sub]
                mi = mi0 + sub
                gt = p.emit("scalar", lambda e, bk=bk, mi=mi: e.activation(
                    g_bf[:, :], bank[bk][:, 0:256], AF.Silu,
                    scale=sg_c[:, mi:mi + 1]), waits=[lastmm, prev_ft], inc="SE")
                ut = p.emit("vector", lambda e, bk=bk, mi=mi: e.tensor_scalar_mul(
                    u_bf[:, :], bank[bk][:, 256:512], su_c[:, mi:mi + 1]),
                    waits=[lastmm, gt], inc="VE")
                ft = p.emit("vector", lambda e, mi=mi: e.tensor_mul(
                    ubuf[:, mi * 256:(mi + 1) * 256], g_bf[:, :], u_bf[:, :]),
                    waits=[gt], inc="VE")
                bank_free[bk] = ft
                prev_ft = ft
        p.barrier()

        slab_free = [None, None]
        for kq in range(16):
            r = kq % 2
            sl = wsl[r]
            sr = p.emit("gpsimd", lambda e, sl=sl, kq=kq: e.dma_start(
                out=sl[:, 0:8192].rearrange("p (kt m) -> p kt m", m=2048),
                in_=ins["wd"][kq * 512:(kq + 1) * 512, :].rearrange(
                    "(kt p) m -> p kt m", p=128)),
                waits=[slab_free[r]], inc="WL", inc_by=16)
            lastmm = None
            for k4 in range(4):
                k = kq * 4 + k4
                for mo in range(16):
                    bk, half = mo // 2, mo % 2
                    lastmm = p.emit("tensor", lambda e, sl=sl, k4=k4, mo=mo, k=k, bk=bk, half=half: e.matmul(
                        bank[bk][:, half * 256:(half + 1) * 256],
                        sl[:, k4 * 2048 + mo * 128: k4 * 2048 + (mo + 1) * 128],
                        ubuf[:, k * 256:(k + 1) * 256],
                        start=(k == 0 and half == 0),
                        stop=(k == 63 and half == 1)),
                        waits=[sr], inc="PE")
            slab_free[r] = lastmm
        dn_done = p.tok("PE")
        for mo in range(16):
            bk, half = mo // 2, mo % 2
            p.emit("vector", lambda e, mo=mo, bk=bk, half=half, fb=fb: e.scalar_tensor_tensor(
                xblk[:, mo * TB + fb * 256: mo * TB + fb * 256 + 256],
                bank[bk][:, half * 256:(half + 1) * 256], sd_c[:, mo:mo + 1],
                xblk[:, mo * TB + fb * 256: mo * TB + fb * 256 + 256],
                ALU.mult, ALU.add), waits=[dn_done], inc="VE")
        p.barrier()
    return p.tok("VE")



def build_l1():
    nc = bass.Bass("TRN2", target_bir_lowering=False, debug=False)
    x_ap = nc.dram_tensor("x", [D, CHUNK], F32, kind="ExternalInput").ap()
    wkv_ap = nc.dram_tensor("wkv", [D, 2 * D], F32, kind="ExternalInput").ap()
    sums_ap = nc.dram_tensor("sums", [128, 32], F32, kind="ExternalOutput").ap()

    with ExitStack() as stack:
        bu = alloc_common(nc, stack, "l1")
        e = stack.enter_context
        bu["ks"] = e(nc.sbuf_tensor("ks", [128, 32], F32))
        p = Prog(nc)
        emit_consts(p, bu)
        p.emit("vector", lambda e2: e2.memset(bu["ks"][:, :], 0.0), inc="VE")

        xblk, ln_bf, tsc, wsl, bank = (
            bu["xblk"], bu["ln_bf"], bu["tsc"], bu["wsl"], bu["bank"])
        xsrc = x_ap.rearrange("(dt p) t -> p dt t", p=128)
        tmin = tsc[:, 0:512]
        texp = tsc[:, 512:1024]
        phik = tsc[:, 1024:1536]
        vvv = tsc[:, 1536:2048]
        colk = tsc[:, 2048:2049]
        colv = tsc[:, 2049:2050]

        for blk in range(NBLK):
            xld = p.emit("sync", lambda e2, blk=blk: e2.dma_start(
                out=xblk[:, :].rearrange("p (dt t) -> p dt t", t=TB),
                in_=xsrc[:, :, blk * TB:(blk + 1) * TB]),
                waits=[p.tok("VE")], inc="XL", inc_by=16)
            ln_tok = emit_ln(p, bu, xld)
            p.barrier()

            slab_free = [None, None]
            prev_phi = None
            for dt in range(DT):
                r = dt % 2
                sl = wsl[r]
                p.emit("gpsimd", lambda e2, sl=sl, dt=dt: e2.dma_start(
                    out=sl[:, 0:2048].rearrange("p (kt m) -> p kt m", m=128),
                    in_=wkv_ap[:, dt * 128:(dt + 1) * 128].rearrange(
                        "(kt p) m -> p kt m", p=128)),
                    waits=[slab_free[r]], inc="WL", inc_by=16)
                sr = p.emit("gpsimd", lambda e2, sl=sl, dt=dt: e2.dma_start(
                    out=sl[:, 2048:4096].rearrange("p (kt m) -> p kt m", m=128),
                    in_=wkv_ap[:, D + dt * 128: D + (dt + 1) * 128].rearrange(
                        "(kt p) m -> p kt m", p=128)),
                    inc="WL", inc_by=16)
                chain_gate = p.tok("VE")
                lastmm = None
                for ch in range(2):
                    for kt in range(DT):
                        p.emit("tensor", lambda e2, sl=sl, kt=kt, ch=ch: e2.matmul(
                            bank[ch][:, :], sl[:, kt * 128:(kt + 1) * 128],
                            ln_bf[:, kt * TB + ch * 512: kt * TB + ch * 512 + 512],
                            start=(kt == 0), stop=(kt == DT - 1)),
                            waits=[sr, ln_tok, chain_gate if kt == 0 else None],
                            inc="PE")
                        lastmm = p.emit("tensor", lambda e2, sl=sl, kt=kt, ch=ch: e2.matmul(
                            bank[2 + ch][:, :],
                            sl[:, 2048 + kt * 128: 2048 + (kt + 1) * 128],
                            ln_bf[:, kt * TB + ch * 512: kt * TB + ch * 512 + 512],
                            start=(kt == 0), stop=(kt == DT - 1)),
                            inc="PE")
                slab_free[r] = lastmm
                for ch in range(2):
                    t1 = p.emit("vector", lambda e2, ch=ch: e2.tensor_scalar_min(
                        tmin, bank[ch][:, :], 0.0),
                        waits=[lastmm, prev_phi], inc="VE")
                    t2 = p.emit("scalar", lambda e2: e2.activation(
                        texp, tmin, AF.Exp), waits=[t1, prev_phi], inc="SE")
                    t3 = p.emit("vector", lambda e2, ch=ch: e2.scalar_tensor_tensor(
                        phik, bank[ch][:, :], 0.0, texp, ALU.max, ALU.add),
                        waits=[t2], inc="VE")
                    prev_phi = t3
                    t4 = p.emit("vector", lambda e2, ch=ch: e2.tensor_copy(
                        vvv, bank[2 + ch][:, :]), waits=[lastmm], inc="VE")
                    t5 = p.emit("vector", lambda e2: e2.reduce_sum(
                        colk, phik, axis=AX.X), inc="VE")
                    p.emit("vector", lambda e2: e2.tensor_mul(vvv, phik, vvv), inc="VE")
                    t6 = p.emit("vector", lambda e2: e2.reduce_sum(
                        colv, vvv, axis=AX.X), inc="VE")
                    p.emit("vector", lambda e2, dt=dt: e2.tensor_add(
                        bu["ks"][:, dt:dt + 1], bu["ks"][:, dt:dt + 1], colk), inc="VE")
                    p.emit("vector", lambda e2, dt=dt: e2.tensor_add(
                        bu["ks"][:, 16 + dt:17 + dt], bu["ks"][:, 16 + dt:17 + dt],
                        colv), inc="VE")
            p.barrier()

        st = p.emit("scalar", lambda e2: e2.dma_start(
            out=sums_ap, in_=bu["ks"][:, :]),
            waits=[p.tok("VE")], inc="ST", inc_by=16)
        p.emit("sync", None, waits=[st])
        p.finalize(stack)
    return nc



def build_l2():
    nc = bass.Bass("TRN2", target_bir_lowering=False, debug=False)
    ins = {}
    for nm, shape, dt_ in (
            ("x", [D, CHUNK], F32), ("wqkv", [D, 3 * D], F32),
            ("wout", [D, D], F32), ("wg", [D, INTER], I8),
            ("wu", [D, INTER], I8), ("wd", [INTER, D], I8),
            ("sg", [128, 64], F32), ("su", [128, 64], F32), ("sd", [128, 16], F32),
            ("xp", [D, 96], F32), ("dtp", [DTR, D], F32), ("dtb", [128, 16], F32),
            ("alog", [128, 256], F32), ("dv", [128, 16], F32),
            ("koff", [3, 128, 32], F32)):
        ins[nm] = nc.dram_tensor(nm, shape, dt_, kind="ExternalInput").ap()
    outs = {}
    for nm, shape, dt_ in (("xloc", [D, CHUNK], F32), ("scum", [D, CHUNK], F32),
                           ("cexp", [NST, CHUNK], BF16), ("eexp", [128, 256], F32),
                           ("dsum", [128, 16], F32)):
        outs[nm] = nc.dram_tensor(nm, shape, dt_, kind="ExternalOutput").ap()

    with ExitStack() as stack:
        bu = alloc_common(nc, stack, "l2")
        e = stack.enter_context
        for nm, sh, dt_ in (("sg_c", [128, 64], F32), ("su_c", [128, 64], F32),
                            ("sd_c", [128, 16], F32), ("dtb_c", [128, 16], F32),
                            ("dv_c", [128, 16], F32), ("napos", [128, 256], F32),
                            ("kvcar", [128, 32], F32), ("scar", [128, 16], F32),
                            ("stcar", [128, 256], F32), ("koi", [128, 96], F32),
                            ("xps", [128, DT * 96], BF16), ("dtps", [64, D], BF16),
                            ("xdl", [64, TB], BF16), ("bcr", [32, TB], BF16),
                            ("brow", [1, 2 * TB], BF16)):
            bu[nm] = e(nc.sbuf_tensor(nm, sh, dt_))

        p = Prog(nc)
        emit_consts(p, bu)
        for nm in ("stcar", "scar"):
            p.emit("vector", lambda e2, nm=nm: e2.memset(bu[nm][:, :], 0.0), inc="VE")

        for src, dst in (("sg", "sg_c"), ("su", "su_c"), ("sd", "sd_c"),
                         ("dtb", "dtb_c"), ("dv", "dv_c"), ("alog", "napos")):
            p.emit("sync", lambda e2, src=src, dst=dst: e2.dma_start(
                out=bu[dst][:, :], in_=ins[src]), inc="XL", inc_by=16)
        p.emit("sync", lambda e2: e2.dma_start(
            out=bu["koi"][:, :].rearrange("p (s c) -> p s c", c=32),
            in_=ins["koff"].rearrange("s p c -> p s c")), inc="XL", inc_by=16)
        p.emit("gpsimd", lambda e2: e2.dma_start(
            out=bu["xps"][:, :].rearrange("p (kt m) -> p kt m", m=96),
            in_=ins["xp"].rearrange("(kt p) m -> p kt m", p=128)), inc="WL", inc_by=16)
        p.emit("gpsimd", lambda e2: e2.dma_start(
            out=bu["dtps"][:, :], in_=ins["dtp"]), inc="WL", inc_by=16)
        ld = p.tok("XL")
        s1 = p.emit("scalar", lambda e2: e2.activation(
            bu["napos"][:, :], bu["napos"][:, :], AF.Exp), waits=[ld], inc="SE")
        p.emit("vector", lambda e2: e2.tensor_scalar_mul(
            bu["napos"][:, :], bu["napos"][:, :], -1.0), waits=[s1], inc="VE")
        p.emit("vector", lambda e2: e2.tensor_add(
            bu["kvcar"][:, :], bu["koi"][:, 0:32], bu["koi"][:, 32:64]),
            waits=[ld], inc="VE")
        p.emit("vector", lambda e2: e2.tensor_add(
            bu["kvcar"][:, :], bu["kvcar"][:, :], bu["koi"][:, 64:96]), inc="VE")
        p.barrier()

        xblk, ln_bf, ubuf, tsc, tbf = (
            bu["xblk"], bu["ln_bf"], bu["ubuf"], bu["tsc"], bu["tbf"])
        wsl, bank = bu["wsl"], bu["bank"]
        xsrc = ins["x"].rearrange("(dt p) t -> p dt t", p=128)
        xloc_dst = outs["xloc"].rearrange("(dt p) t -> p dt t", p=128)

        phiq = tsc[:, 0:512]
        phik = tsc[:, 512:1024]
        vkv = tsc[:, 1024:1536]
        kcd = tsc[:, 1536:2048]
        kvn = tsc[:, 2048:2560]
        tmin = tbf[:, 4 * TB:4 * TB + 512]
        texp = tbf[:, 4 * TB + 512:5 * TB]

        prev_export = None
        for blk in range(NBLK):
            xld = p.emit("sync", lambda e2, blk=blk: e2.dma_start(
                out=xblk[:, :].rearrange("p (dt t) -> p dt t", t=TB),
                in_=xsrc[:, :, blk * TB:(blk + 1) * TB]),
                waits=[prev_export, p.tok("VE")], inc="XL", inc_by=16)

            ln_tok = emit_ln(p, bu, xld)
            p.barrier()

            slab_free = [None, None]
            prev_chain = None
            for dt in range(DT):
                r = dt % 2
                sl = wsl[r]
                sr = None
                for j, cc in enumerate((dt * 128, D + dt * 128, 2 * D + dt * 128)):
                    sr = p.emit("gpsimd", lambda e2, sl=sl, j=j, cc=cc: e2.dma_start(
                        out=sl[:, j * 2048:(j + 1) * 2048].rearrange(
                            "p (kt m) -> p kt m", m=128),
                        in_=ins["wqkv"][:, cc:cc + 128].rearrange(
                            "(kt p) m -> p kt m", p=128)),
                        waits=[slab_free[r]] if j == 0 else (),
                        inc="WL", inc_by=16)
                chain_gate = prev_chain
                lastmm = None
                for j in range(3):
                    for ch in range(2):
                        for kt in range(DT):
                            lastmm = p.emit("tensor", lambda e2, sl=sl, j=j, ch=ch, kt=kt: e2.matmul(
                                bank[2 * j + ch][:, :],
                                sl[:, j * 2048 + kt * 128: j * 2048 + (kt + 1) * 128],
                                ln_bf[:, kt * TB + ch * 512: kt * TB + ch * 512 + 512],
                                start=(kt == 0), stop=(kt == DT - 1)),
                                waits=[sr, ln_tok, chain_gate if kt == 0 else None],
                                inc="PE")
                slab_free[r] = lastmm
                for ch in range(2):
                    for j, dst in ((0, phiq), (1, phik)):
                        t1 = p.emit("vector", lambda e2, j=j, ch=ch: e2.tensor_scalar_min(
                            tmin, bank[2 * j + ch][:, :], 0.0),
                            waits=[lastmm, prev_chain], inc="VE")
                        t2 = p.emit("scalar", lambda e2: e2.activation(
                            texp, tmin, AF.Exp), waits=[t1, prev_chain], inc="SE")
                        t3 = p.emit("vector", lambda e2, j=j, ch=ch, dst=dst: e2.scalar_tensor_tensor(
                            dst, bank[2 * j + ch][:, :], 0.0, texp,
                            ALU.max, ALU.add), waits=[t2], inc="VE")
                        prev_chain = t3
                    p.emit("vector", lambda e2, ch=ch: e2.tensor_copy(
                        vkv, bank[4 + ch][:, :]), waits=[lastmm], inc="VE")
                    p.emit("vector", lambda e2: e2.tensor_mul(vkv, phik, vkv), inc="VE")
                    sck = p.emit("vector", lambda e2, dt=dt: e2.tensor_tensor_scan(
                        kcd, bu["onest"][:, 0:512], phik,
                        bu["kvcar"][:, dt:dt + 1], ALU.mult, ALU.add), inc="V2")
                    sckv = p.emit("vector", lambda e2, dt=dt: e2.tensor_tensor_scan(
                        kvn, bu["onest"][:, 0:512], vkv,
                        bu["kvcar"][:, 16 + dt:17 + dt], ALU.mult, ALU.add), inc="V2")
                    p.emit("vector", lambda e2, dt=dt: e2.tensor_copy(
                        bu["kvcar"][:, dt:dt + 1], kcd[:, 511:512]),
                        waits=[sck], inc="VE")
                    p.emit("vector", lambda e2, dt=dt: e2.tensor_copy(
                        bu["kvcar"][:, 16 + dt:17 + dt], kvn[:, 511:512]),
                        waits=[sckv], inc="VE")
                    p.emit("vector", lambda e2: e2.tensor_mul(kvn, phiq, kvn),
                        waits=[sckv], inc="VE")
                    p.emit("vector", lambda e2: e2.tensor_mul(kcd, phiq, kcd),
                        waits=[sck], inc="VE")
                    p.emit("vector", lambda e2: e2.tensor_scalar_add(
                        kcd, kcd, A_EPS), inc="VE")
                    p.emit("vector", lambda e2: e2.reciprocal(kcd, kcd), inc="VE")
                    prev_chain = p.emit("vector", lambda e2, dt=dt, ch=ch: e2.tensor_mul(
                        ubuf[:, dt * 1024 + ch * 512: dt * 1024 + ch * 512 + 512],
                        kvn, kcd), inc="VE")
            attn_ev = p.tok("VE")
            p.barrier()

            slab_free = [None, None]
            prev_ev = [None] * 4
            for m in range(DT):
                r = m % 2
                sl = wsl[r]
                sr = p.emit("gpsimd", lambda e2, sl=sl, m=m: e2.dma_start(
                    out=sl[:, 0:2048].rearrange("p (kt mm) -> p kt mm", mm=128),
                    in_=ins["wout"][:, m * 128:(m + 1) * 128].rearrange(
                        "(kt p) mm -> p kt mm", p=128)),
                    waits=[slab_free[r]], inc="WL", inc_by=16)
                lastmm = None
                for ch in range(2):
                    bk = 2 * r + ch
                    for kt in range(DT):
                        lastmm = p.emit("tensor", lambda e2, sl=sl, ch=ch, kt=kt, bk=bk: e2.matmul(
                            bank[bk][:, :], sl[:, kt * 128:(kt + 1) * 128],
                            ubuf[:, kt * 1024 + ch * 512: kt * 1024 + ch * 512 + 512],
                            start=(kt == 0), stop=(kt == DT - 1)),
                            waits=[sr, attn_ev, prev_ev[bk]], inc="PE")
                slab_free[r] = lastmm
                for ch in range(2):
                    bk = 2 * r + ch
                    prev_ev[bk] = p.emit("vector", lambda e2, m=m, ch=ch, bk=bk: e2.scalar_tensor_tensor(
                        xblk[:, m * TB + ch * 512: m * TB + ch * 512 + 512],
                        bank[bk][:, :], HD_SCALE,
                        xblk[:, m * TB + ch * 512: m * TB + ch * 512 + 512],
                        ALU.mult, ALU.add), waits=[lastmm], inc="VE")
            p.barrier()

            ln_tok = emit_ln(p, bu, None)
            p.barrier()
            emit_ffn(p, bu, ins, ln_tok)

            ln_tok = emit_ln(p, bu, None)
            p.barrier()

            lastmm = None
            for ch in range(2):
                for kt in range(DT):
                    lastmm = p.emit("tensor", lambda e2, ch=ch, kt=kt: e2.matmul(
                        bank[6 + ch][0:96, :], bu["xps"][:, kt * 96:(kt + 1) * 96],
                        ln_bf[:, kt * TB + ch * 512: kt * TB + ch * 512 + 512],
                        start=(kt == 0), stop=(kt == DT - 1)),
                        waits=[ln_tok], inc="PE")
            for ch in range(2):
                p.emit("vector", lambda e2, ch=ch: e2.tensor_copy(
                    bu["xdl"][:, ch * 512:(ch + 1) * 512], bank[6 + ch][0:64, :]),
                    waits=[lastmm], inc="VE")
                p.emit("vector", lambda e2, ch=ch: e2.tensor_copy(
                    bu["bcr"][:, ch * 512:(ch + 1) * 512], bank[6 + ch][64:96, :]),
                    waits=[lastmm], inc="VE")
            xdl_tok = p.tok("VE")
            p.emit("scalar", lambda e2, blk=blk: e2.dma_start(
                out=outs["cexp"][:, blk * TB:(blk + 1) * TB],
                in_=bu["bcr"][16:32, :]), waits=[xdl_tok], inc="ST", inc_by=16)
            p.barrier()

            zt = tsc[:, 0:TB]
            pt = tsc[:, TB:2 * TB]
            scm = tsc[:, 2048:3072]
            scm_exp = None
            for m in range(DT):
                mm1 = p.emit("tensor", lambda e2, m=m: e2.matmul(
                    bank[m % 2][:, :], bu["dtps"][0:64, m * 128:(m + 1) * 128],
                    bu["xdl"][:, 0:512], start=True, stop=True),
                    waits=[xdl_tok, p.tok("VE") if m >= 2 else None], inc="PE")
                mm2 = p.emit("tensor", lambda e2, m=m: e2.matmul(
                    bank[2 + m % 2][:, :], bu["dtps"][0:64, m * 128:(m + 1) * 128],
                    bu["xdl"][:, 512:1024], start=True, stop=True), inc="PE")
                p.emit("vector", lambda e2, m=m: e2.tensor_scalar(
                    zt[:, 0:512], bank[m % 2][:, :], bu["dtb_c"][:, m:m + 1], 30.0,
                    ALU.add, ALU.min), waits=[mm2], inc="VE")
                t1 = p.emit("vector", lambda e2, m=m: e2.tensor_scalar(
                    zt[:, 512:1024], bank[2 + m % 2][:, :], bu["dtb_c"][:, m:m + 1],
                    30.0, ALU.add, ALU.min), inc="VE")
                t2 = p.emit("scalar", lambda e2: e2.activation(
                    pt, zt, AF.Exp), waits=[t1], inc="SE")
                t3 = p.emit("scalar", lambda e2, m=m: e2.activation(
                    ubuf[:, m * TB:(m + 1) * TB], pt, AF.Ln, bias=bu["cst"][:, 0:1]),
                    waits=[t2], inc="SE")
                p.emit("vector", lambda e2, m=m: e2.scalar_tensor_tensor(
                    xblk[:, m * TB:(m + 1) * TB], ln_bf[:, m * TB:(m + 1) * TB],
                    bu["dv_c"][:, m:m + 1], xblk[:, m * TB:(m + 1) * TB],
                    ALU.mult, ALU.add), waits=[t3], inc="VE")
                sc = p.emit("vector", lambda e2, m=m: e2.tensor_tensor_scan(
                    scm, bu["onest"][:, :], ubuf[:, m * TB:(m + 1) * TB],
                    bu["scar"][:, m:m + 1], ALU.mult, ALU.add),
                    waits=[scm_exp], inc="V2")
                p.emit("vector", lambda e2, m=m: e2.tensor_copy(
                    bu["scar"][:, m:m + 1], scm[:, TB - 1:TB]),
                    waits=[sc], inc="VE")
                scm_exp = p.emit("scalar", lambda e2, m=m, blk=blk: e2.dma_start(
                    out=outs["scum"][m * 128:(m + 1) * 128, blk * TB:(blk + 1) * TB],
                    in_=scm[:, :]), waits=[sc], inc="ST", inc_by=16)
                p.emit("vector", lambda e2, m=m: e2.tensor_mul(
                    ln_bf[:, m * TB:(m + 1) * TB], ubuf[:, m * TB:(m + 1) * TB],
                    ln_bf[:, m * TB:(m + 1) * TB]), inc="VE")
            p.barrier()

            dA = [tbf[:, 0:TB], tbf[:, TB:2 * TB]]
            dBu = tbf[:, 2 * TB:3 * TB]
            stt_ = tbf[:, 3 * TB:4 * TB]
            tmpy = tbf[:, 4 * TB:5 * TB]
            prev_bmm = None
            prev_cmm = None
            prev_scan = [None, None]
            for n in range(NST):
                bl = p.emit("sync", lambda e2, n=n: e2.dma_start(
                    out=bu["brow"][:, 0:TB], in_=bu["bcr"][n:n + 1, :]),
                    waits=[prev_bmm], inc="XL", inc_by=16)
                cl = p.emit("sync", lambda e2, n=n: e2.dma_start(
                    out=bu["brow"][:, TB:2 * TB], in_=bu["bcr"][16 + n:17 + n, :]),
                    waits=[prev_cmm], inc="XL", inc_by=16)
                bmm, cmm = [], []
                for ch in range(2):
                    bmm.append(p.emit("tensor", lambda e2, ch=ch: e2.matmul(
                        bank[4 + ch][:, :], bu["onesrb"][0:1, :],
                        bu["brow"][0:1, ch * 512:(ch + 1) * 512],
                        start=True, stop=True),
                        waits=[bl, p.tok("VE") if n > 0 else None], inc="PE"))
                    cmm.append(p.emit("tensor", lambda e2, ch=ch: e2.matmul(
                        bank[6 + ch][:, :], bu["onesrb"][0:1, :],
                        bu["brow"][0:1, TB + ch * 512: TB + ch * 512 + 512],
                        start=True, stop=True), waits=[cl], inc="PE"))
                prev_bmm, prev_cmm = bmm[-1], cmm[-1]
                bc_done = p.tok("PE")
                for dt in range(DT):
                    col = dt * 16 + n
                    r = dt % 2
                    sa = p.emit("scalar", lambda e2, dt=dt, col=col, r=r: e2.activation(
                        dA[r], ubuf[:, dt * TB:(dt + 1) * TB], AF.Exp,
                        scale=bu["napos"][:, col:col + 1]),
                        waits=[prev_scan[r]], inc="SE")
                    for ch in range(2):
                        p.emit("vector", lambda e2, dt=dt, ch=ch: e2.tensor_tensor(
                            dBu[:, ch * 512:(ch + 1) * 512],
                            ln_bf[:, dt * TB + ch * 512: dt * TB + ch * 512 + 512],
                            bank[4 + ch][:, :], ALU.mult),
                            waits=[bc_done], inc="VE")
                    sc = p.emit("vector", lambda e2, col=col, r=r: e2.tensor_tensor_scan(
                        stt_, dA[r], dBu, bu["stcar"][:, col:col + 1],
                        ALU.mult, ALU.add), waits=[sa], inc="V2")
                    prev_scan[r] = sc
                    p.emit("vector", lambda e2, col=col: e2.tensor_copy(
                        bu["stcar"][:, col:col + 1], stt_[:, TB - 1:TB]),
                        waits=[sc], inc="VE")
                    for ch in range(2):
                        p.emit("vector", lambda e2, ch=ch: e2.tensor_tensor(
                            tmpy[:, ch * 512:(ch + 1) * 512],
                            stt_[:, ch * 512:(ch + 1) * 512],
                            bank[6 + ch][:, :], ALU.mult), waits=[sc], inc="VE")
                    p.emit("vector", lambda e2, dt=dt: e2.tensor_add(
                        xblk[:, dt * TB:(dt + 1) * TB],
                        xblk[:, dt * TB:(dt + 1) * TB], tmpy), inc="VE")
            p.barrier()

            prev_export = p.emit("scalar", lambda e2, blk=blk: e2.dma_start(
                out=xloc_dst[:, :, blk * TB:(blk + 1) * TB],
                in_=xblk[:, :].rearrange("p (dt t) -> p dt t", t=TB)),
                waits=[p.tok("VE")], inc="ST", inc_by=16)

        p.emit("scalar", lambda e2: e2.dma_start(
            out=outs["eexp"], in_=bu["stcar"][:, :]),
            waits=[p.tok("VE")], inc="ST", inc_by=16)
        p.emit("scalar", lambda e2: e2.dma_start(
            out=outs["dsum"], in_=bu["scar"][:, :]), inc="ST", inc_by=16)
        p.emit("sync", None, waits=[p.tok("ST")])
        p.finalize(stack)
    return nc



def build_l3():
    nc = bass.Bass("TRN2", target_bir_lowering=False, debug=False)
    ins = {}
    for nm, shape, dt_ in (
            ("xloc", [D, CHUNK], F32), ("scum", [D, CHUNK], F32),
            ("cexp", [NST, CHUNK], BF16), ("epred", [3, 128, 256], F32),
            ("dsr", [3, 128, 256], F32), ("alog", [128, 256], F32),
            ("wg", [D, INTER], I8), ("wu", [D, INTER], I8), ("wd", [INTER, D], I8),
            ("sg", [128, 64], F32), ("su", [128, 64], F32), ("sd", [128, 16], F32)):
        ins[nm] = nc.dram_tensor(nm, shape, dt_, kind="ExternalInput").ap()
    y_ap = nc.dram_tensor("y", [D, CHUNK], F32, kind="ExternalOutput").ap()

    with ExitStack() as stack:
        bu = alloc_common(nc, stack, "l3")
        e = stack.enter_context
        for nm, sh in (("sg_c", [128, 64]), ("su_c", [128, 64]), ("sd_c", [128, 16]),
                       ("napos", [128, 256]), ("s0", [128, 256]),
                       ("eds", [128, 6 * 256])):
            bu[nm] = e(nc.sbuf_tensor(nm, sh, F32))
        bu["brow"] = e(nc.sbuf_tensor("brow", [1, TB], BF16))

        p = Prog(nc)
        emit_consts(p, bu)
        for src, dst in (("sg", "sg_c"), ("su", "su_c"), ("sd", "sd_c"),
                         ("alog", "napos")):
            p.emit("sync", lambda e2, src=src, dst=dst: e2.dma_start(
                out=bu[dst][:, :], in_=ins[src]), inc="XL", inc_by=16)
        for s in range(3):
            p.emit("sync", lambda e2, s=s: e2.dma_start(
                out=bu["eds"][:, s * 256:(s + 1) * 256], in_=ins["epred"][s]),
                inc="XL", inc_by=16)
            p.emit("sync", lambda e2, s=s: e2.dma_start(
                out=bu["eds"][:, (3 + s) * 256:(4 + s) * 256], in_=ins["dsr"][s]),
                inc="XL", inc_by=16)
        ld = p.tok("XL")
        s1 = p.emit("scalar", lambda e2: e2.activation(
            bu["napos"][:, :], bu["napos"][:, :], AF.Exp), waits=[ld], inc="SE")
        p.emit("vector", lambda e2: e2.tensor_scalar_mul(
            bu["napos"][:, :], bu["napos"][:, :], -1.0), waits=[s1], inc="VE")
        p.emit("vector", lambda e2: e2.memset(bu["s0"][:, :], 0.0), inc="VE")
        tA = bu["tsc"][:, 0:256]
        for s in range(3):
            v1 = p.emit("vector", lambda e2, s=s: e2.tensor_mul(
                tA, bu["napos"][:, :], bu["eds"][:, (3 + s) * 256:(4 + s) * 256]),
                waits=[ld], inc="VE")
            s2 = p.emit("scalar", lambda e2: e2.activation(
                tA, tA, AF.Exp), waits=[v1], inc="SE")
            p.emit("vector", lambda e2: e2.tensor_mul(
                bu["s0"][:, :], bu["s0"][:, :], tA), waits=[s2], inc="VE")
            p.emit("vector", lambda e2, s=s: e2.tensor_add(
                bu["s0"][:, :], bu["s0"][:, :], bu["eds"][:, s * 256:(s + 1) * 256]),
                inc="VE")
        p.barrier()

        xblk, ln_bf, ubuf, tsc, tbf = (
            bu["xblk"], bu["ln_bf"], bu["ubuf"], bu["tsc"], bu["tbf"])
        bank = bu["bank"]
        xsrc = ins["xloc"].rearrange("(dt p) t -> p dt t", p=128)
        y_dst = y_ap.rearrange("(dt p) t -> p dt t", p=128)

        prev_export = None
        for blk in range(NBLK):
            xld = p.emit("sync", lambda e2, blk=blk: e2.dma_start(
                out=xblk[:, :].rearrange("p (dt t) -> p dt t", t=TB),
                in_=xsrc[:, :, blk * TB:(blk + 1) * TB]),
                waits=[prev_export, p.tok("VE")], inc="XL", inc_by=16)

            prev_cp = None
            prev_mm = None
            for n in range(NST):
                cl = p.emit("sync", lambda e2, n=n, blk=blk: e2.dma_start(
                    out=bu["brow"][:, :],
                    in_=ins["cexp"][n:n + 1, blk * TB:(blk + 1) * TB]),
                    waits=[prev_mm], inc="XL", inc_by=16)
                for ch in range(2):
                    prev_mm = p.emit("tensor", lambda e2, ch=ch: e2.matmul(
                        bank[6][:, :], bu["onesrb"][0:1, :],
                        bu["brow"][0:1, ch * 512:(ch + 1) * 512],
                        start=True, stop=True), waits=[cl, prev_cp], inc="PE")
                    prev_cp = p.emit("vector", lambda e2, n=n, ch=ch: e2.tensor_copy(
                        ubuf[:, n * TB + ch * 512: n * TB + ch * 512 + 512],
                        bank[6][:, :]), waits=[prev_mm], inc="VE")
            p.barrier()

            scm = [tsc[:, 0:TB], tsc[:, TB:2 * TB]]
            cp = [tbf[:, 0:TB], tbf[:, TB:2 * TB]]
            tt = tbf[:, 2 * TB:3 * TB]
            sld = [None, None]
            prev_v1 = [None, None]
            for dt in range(DT):
                r = dt % 2
                sld[r] = p.emit("sync", lambda e2, dt=dt, blk=blk, r=r: e2.dma_start(
                    out=scm[r][:, :],
                    in_=ins["scum"][dt * 128:(dt + 1) * 128,
                                    blk * TB:(blk + 1) * TB]),
                    waits=[p.tok("VE") if dt >= 2 else None], inc="XL", inc_by=16)
                for n in range(NST):
                    col = dt * 16 + n
                    q = n % 2
                    s1 = p.emit("scalar", lambda e2, r=r, col=col, q=q: e2.activation(
                        cp[q][:, :], scm[r][:, :], AF.Exp,
                        scale=bu["napos"][:, col:col + 1]),
                        waits=[sld[r], prev_v1[q]], inc="SE")
                    v1 = p.emit("vector", lambda e2, n=n, col=col, q=q: e2.scalar_tensor_tensor(
                        tt, cp[q][:, :], bu["s0"][:, col:col + 1],
                        ubuf[:, n * TB:(n + 1) * TB], ALU.mult, ALU.mult),
                        waits=[s1], inc="VE")
                    prev_v1[q] = v1
                    p.emit("vector", lambda e2, dt=dt: e2.tensor_add(
                        xblk[:, dt * TB:(dt + 1) * TB],
                        xblk[:, dt * TB:(dt + 1) * TB], tt), inc="VE")
            p.barrier()

            ln_tok = emit_ln(p, bu, None)
            p.barrier()
            emit_ffn(p, bu, ins, ln_tok)

            prev_export = p.emit("scalar", lambda e2, blk=blk: e2.dma_start(
                out=y_dst[:, :, blk * TB:(blk + 1) * TB],
                in_=xblk[:, :].rearrange("p (dt t) -> p dt t", t=TB)),
                waits=[p.tok("VE")], inc="ST", inc_by=16)

        p.emit("sync", None, waits=[p.tok("ST")])
        p.finalize(stack)
    return nc



def _col128(v):
    return np.ascontiguousarray(np.asarray(v, np.float32).reshape(-1, 128).T)


def kernel(L0_attn_a_x, L0_attn_a_W_qkv, L0_attn_a_W_out,
           L0_attn_b_x_proj_weight, L0_attn_b_dt_proj_weight,
           L0_attn_b_dt_proj_bias, L0_attn_b_A_log, L0_attn_b_D,
           L0_ffn_w_gate, L0_ffn_w_up, L0_ffn_w_down,
           L0_ffn_s_gate, L0_ffn_s_up, L0_ffn_s_down):
    f32 = np.float32
    x = np.asarray(L0_attn_a_x, f32)
    x_cm = np.ascontiguousarray(x.transpose(0, 2, 1))
    wqkv_t = np.ascontiguousarray(np.asarray(L0_attn_a_W_qkv, f32).T)
    wout_t = np.ascontiguousarray(np.asarray(L0_attn_a_W_out, f32).T)
    wkv = np.ascontiguousarray(wqkv_t[:, D:])
    wg = np.ascontiguousarray(np.asarray(L0_ffn_w_gate))
    wu = np.ascontiguousarray(np.asarray(L0_ffn_w_up))
    wd = np.ascontiguousarray(np.asarray(L0_ffn_w_down))
    sg_c, su_c, sd_c = _col128(L0_ffn_s_gate), _col128(L0_ffn_s_up), _col128(L0_ffn_s_down)
    dtb_c, dv_c = _col128(L0_attn_b_dt_proj_bias), _col128(L0_attn_b_D)
    alog_cm = np.ascontiguousarray(
        np.asarray(L0_attn_b_A_log, f32).reshape(16, 128, 16)
        .transpose(1, 0, 2).reshape(128, 256))
    xp = np.ascontiguousarray(np.asarray(L0_attn_b_x_proj_weight, f32))
    dtp = np.ascontiguousarray(np.asarray(L0_attn_b_dt_proj_weight, f32))

    xs = [np.ascontiguousarray(x_cm[c // 4][:, (c % 4) * CHUNK:(c % 4 + 1) * CHUNK])
          for c in range(NCORE)]

    nc1 = build_l1()
    res1 = run_bass_kernel_spmd(
        nc1, [{"x": xs[c], "wkv": wkv} for c in range(NCORE)],
        core_ids=list(range(NCORE)))
    sums = [np.asarray(res1.results[c]["sums"]) for c in range(NCORE)]

    nc2 = build_l2()
    maps2 = []
    for c in range(NCORE):
        bi, j = c // 4, c % 4
        koff = np.zeros((3, 128, 32), f32)
        for i in range(j):
            koff[i] = sums[bi * 4 + i]
        maps2.append({"x": xs[c], "wqkv": wqkv_t, "wout": wout_t,
                      "wg": wg, "wu": wu, "wd": wd, "sg": sg_c, "su": su_c,
                      "sd": sd_c, "xp": xp, "dtp": dtp, "dtb": dtb_c,
                      "alog": alog_cm, "dv": dv_c, "koff": koff})
    res2 = run_bass_kernel_spmd(nc2, maps2, core_ids=list(range(NCORE)))

    nc3 = build_l3()
    maps3 = []
    for c in range(NCORE):
        bi, j = c // 4, c % 4
        epred = np.zeros((3, 128, 256), f32)
        dsr = np.zeros((3, 128, 256), f32)
        for i in range(j):
            rc = res2.results[bi * 4 + i]
            epred[i] = np.asarray(rc["eexp"])
            dsr[i] = np.repeat(np.asarray(rc["dsum"]), 16, axis=1)
        rme = res2.results[c]
        maps3.append({"xloc": np.asarray(rme["xloc"]),
                      "scum": np.asarray(rme["scum"]),
                      "cexp": np.asarray(rme["cexp"]),
                      "epred": epred, "dsr": dsr, "alog": alog_cm,
                      "wg": wg, "wu": wu, "wd": wd,
                      "sg": sg_c, "su": su_c, "sd": sd_c})
    res3 = run_bass_kernel_spmd(nc3, maps3, core_ids=list(range(NCORE)))

    out = np.empty((B, L, D), f32)
    for c in range(NCORE):
        bi, j = c // 4, c % 4
        out[bi, j * CHUNK:(j + 1) * CHUNK, :] = np.asarray(res3.results[c]["y"]).T
    return out


# version 06
# speedup vs baseline: 1.2381x; 1.2381x over previous
import numpy as np
from contextlib import ExitStack

import concourse.bass as bass
import concourse.mybir as mybir
from concourse.bass_utils import run_bass_kernel_spmd

F32 = mybir.dt.float32
BF16 = mybir.dt.bfloat16
I8 = mybir.dt.int8
AF = mybir.ActivationFunctionType
ALU = mybir.AluOpType
AX = mybir.AxisListType

NCORE = 8
B, L, D = 2, 8192, 2048
CHUNK = 2048
NBLK, TB = 2, 1024
DT = 16
NST = 16
DTR = 64
INTER = 8192
MI = 64
HD_SCALE = float(128 ** -0.5)
INV_D = 1.0 / 2048.0
LN_EPS = 1e-5
A_EPS = 1e-9

ENGINES = ("tensor", "vector", "scalar", "gpsimd", "sync")


class Prog:

    OWN = {"tensor": "PE", "vector": "VE", "scalar": "SE"}

    def __init__(self, nc):
        self.nc = nc
        self.ops = {e: [] for e in ENGINES}
        self.counts = {}

    def tok(self, sem):
        return (sem, self.counts.get(sem, 0), 0)

    def emit(self, eng, fn, waits=(), inc=None, inc_by=1):
        tok = None
        if inc is not None:
            self.counts[inc] = self.counts.get(inc, 0) + inc_by
            tok = (inc, self.counts[inc], inc_by)
        ws = tuple((w[0], w[1]) for w in waits if w is not None)
        self.ops[eng].append((ws, fn, tok))
        return tok

    def barrier(self):
        toks = [(s, self.counts[s]) for s in ("PE", "VE", "SE", "V2", "WL", "XL", "ST")
                if s in self.counts]
        for e in ENGINES:
            self.ops[e].append((tuple(toks), None, None))

    def finalize(self, stack):
        nc = self.nc
        sems = {name: stack.enter_context(nc.semaphore(name))
                for name in self.counts}
        block = stack.enter_context(nc.Block())

        def mkbody(ops, own):
            def body(eng):
                waited = {}
                for ws, fn, tok in ops:
                    for (s, v) in ws:
                        if s == own or waited.get(s, 0) >= v:
                            continue
                        eng.wait_ge(sems[s], v)
                        waited[s] = v
                    inst = fn(eng) if fn is not None else None
                    if tok is not None and inst is not None:
                        inst.then_inc(sems[tok[0]], tok[2])
            return body

        for engname in ENGINES:
            getattr(block, engname)(mkbody(self.ops[engname], Prog.OWN.get(engname)))



def alloc_common(nc, stack, kind):
    bu = {}
    e = stack.enter_context
    bu["xblk"] = e(nc.sbuf_tensor("xblk", [128, DT * TB], F32))
    bu["ln_bf"] = e(nc.sbuf_tensor("ln_bf", [128, DT * TB], BF16))
    if kind != "l1":
        bu["ubuf"] = e(nc.sbuf_tensor("ubuf", [128, 16384], BF16))
    nw = 4096 if kind == "l1" else 8192
    bu["wsl"] = [e(nc.sbuf_tensor(f"wsl{i}", [128, nw], BF16)) for i in (0, 1)]
    bu["tsc"] = e(nc.sbuf_tensor("tsc", [128, 3072], F32))
    bu["tbf"] = e(nc.sbuf_tensor("tbf", [128, 5120], BF16))
    bu["cst"] = e(nc.sbuf_tensor("cst", [128, 8], F32))
    bu["cstb"] = e(nc.sbuf_tensor("cstb", [128, 2], BF16))
    bu["onesr"] = e(nc.sbuf_tensor("onesr", [1, 128], F32))
    bu["onesrb"] = e(nc.sbuf_tensor("onesrb", [1, 128], BF16))
    bu["rbc"] = e(nc.sbuf_tensor("rbc", [128, TB], BF16))
    bu["nmrbc"] = e(nc.sbuf_tensor("nmrbc", [128, TB], BF16))
    bu["onest"] = e(nc.sbuf_tensor("onest", [128, TB], BF16))
    bu["bank"] = [e(nc.psum_tensor(f"pb{i}", [128, 512], F32)) for i in range(8)]
    return bu


def emit_consts(p, bu):
    cst, cstb = bu["cst"], bu["cstb"]
    for i, v in enumerate([1.0, LN_EPS, -0.5, 30.0, A_EPS, 0.0, 0.0, 0.0]):
        p.emit("vector", lambda e, i=i, v=v: e.memset(cst[:, i:i + 1], v), inc="VE")
    p.emit("vector", lambda e: e.memset(cstb[:, 0:1], 1.0), inc="VE")
    p.emit("vector", lambda e: e.memset(bu["onesr"][:, :], 1.0), inc="VE")
    p.emit("vector", lambda e: e.memset(bu["onesrb"][:, :], 1.0), inc="VE")
    p.emit("vector", lambda e: e.memset(bu["onest"][:, :], 1.0), inc="VE")



def emit_ln(p, bu, xld_tok):
    xblk, ln_bf, tbf = bu["xblk"], bu["ln_bf"], bu["tbf"]
    cst, cstb, onesr = bu["cst"], bu["cstb"], bu["onesr"]
    rbc, nmrbc, bank = bu["rbc"], bu["nmrbc"], bu["bank"]
    srow = bu["tsc"]

    xbf = [tbf[:, 0:TB], tbf[:, TB:2 * TB]]
    sqb = [tbf[:, 2 * TB:3 * TB], tbf[:, 3 * TB:4 * TB]]

    mm_gate = [None, None]
    prev_sq = [None, None]
    for dt in range(DT):
        r = dt % 2
        cp = p.emit("vector",
                    lambda e, dt=dt, r=r: e.tensor_copy(
                        xbf[r][:, :], xblk[:, dt * TB:(dt + 1) * TB]),
                    waits=[xld_tok, mm_gate[r]], inc="VE")
        sq = p.emit("scalar",
                    lambda e, r=r: e.activation(sqb[r][:, :], xbf[r][:, :], AF.Square),
                    waits=[cp, mm_gate[r]], inc="SE")
        last = None
        for ch in range(2):
            p.emit("tensor",
                   lambda e, r=r, ch=ch, dt=dt: e.matmul(
                       bank[4 + ch][0:1, :], cstb[:, 0:1],
                       xbf[r][:, ch * 512:(ch + 1) * 512],
                       start=(dt == 0), stop=(dt == DT - 1)),
                   waits=[cp], inc="PE")
            last = p.emit("tensor",
                          lambda e, r=r, ch=ch, dt=dt: e.matmul(
                              bank[6 + ch][0:1, :], cstb[:, 0:1],
                              sqb[r][:, ch * 512:(ch + 1) * 512],
                              start=(dt == 0), stop=(dt == DT - 1)),
                          waits=[sq], inc="PE")
        mm_gate[r] = last
    stats_done = p.tok("PE")

    r_mean = srow[0:1, 0:512]
    r_rstd = srow[0:1, 512:1024]
    r_nmr = srow[0:1, 1024:1536]
    prev_bcast = None
    for ch in range(2):
        t0 = p.emit("vector", lambda e, ch=ch: e.tensor_scalar_mul(
            r_mean, bank[4 + ch][0:1, :], INV_D),
            waits=[stats_done, prev_bcast], inc="VE")
        p.emit("vector", lambda e: e.tensor_mul(r_nmr, r_mean, r_mean), inc="VE")
        t1 = p.emit("vector", lambda e, ch=ch: e.scalar_tensor_tensor(
            r_rstd, bank[6 + ch][0:1, :], INV_D, r_nmr, ALU.mult, ALU.subtract),
            inc="VE")
        s1 = p.emit("scalar", lambda e: e.activation(
            r_rstd, r_rstd, AF.Ln, bias=cst[0:1, 1:2]), waits=[t1], inc="SE")
        s2 = p.emit("scalar", lambda e: e.activation(
            r_rstd, r_rstd, AF.Exp, scale=cst[0:1, 2:3]), waits=[s1], inc="SE")
        nm = p.emit("vector", lambda e: e.scalar_tensor_tensor(
            r_nmr, r_mean, -1.0, r_rstd, ALU.mult, ALU.mult),
            waits=[s2], inc="VE")
        m1 = p.emit("tensor", lambda e, ch=ch: e.matmul(
            bank[4 + ch][:, :], onesr[0:1, :], r_rstd, start=True, stop=True),
            waits=[nm], inc="PE")
        m2 = p.emit("tensor", lambda e, ch=ch: e.matmul(
            bank[6 + ch][:, :], onesr[0:1, :], r_nmr, start=True, stop=True),
            waits=[nm], inc="PE")
        prev_bcast = m2
        ev1 = p.emit("vector", lambda e, ch=ch: e.tensor_copy(
            rbc[:, ch * 512:(ch + 1) * 512], bank[4 + ch][:, :]),
            waits=[m2], inc="VE")
        ev2 = p.emit("vector", lambda e, ch=ch: e.tensor_copy(
            nmrbc[:, ch * 512:(ch + 1) * 512], bank[6 + ch][:, :]),
            waits=[m2], inc="VE")
    bc_ev = p.tok("VE")

    for dt in range(DT):
        r = dt % 2
        p.emit("vector", lambda e, dt=dt, r=r: e.tensor_copy(
            xbf[r][:, :], xblk[:, dt * TB:(dt + 1) * TB]),
            waits=[bc_ev], inc="VE")
        p.emit("vector", lambda e, r=r: e.tensor_mul(
            sqb[r][:, :], xbf[r][:, :], rbc[:, :]), inc="VE")
        p.emit("vector", lambda e, dt=dt, r=r: e.tensor_add(
            ln_bf[:, dt * TB:(dt + 1) * TB], sqb[r][:, :], nmrbc[:, :]), inc="VE")
    return p.tok("VE")



def emit_ffn(p, bu, ins, ln_tok):
    xblk, ln_bf, ubuf = bu["xblk"], bu["ln_bf"], bu["ubuf"]
    wsl, tbf, bank = bu["wsl"], bu["tbf"], bu["bank"]
    sg_c, su_c, sd_c = bu["sg_c"], bu["su_c"], bu["sd_c"]

    g_bf = tbf[:, 0:256]
    u_bf = tbf[:, 256:512]

    prev_ft = None
    prev_p2 = None
    slab_gate = None
    for fb in range(4):
        slab_free = [slab_gate, slab_gate]
        bank_free = [prev_p2, prev_p2, prev_p2, prev_p2]
        for g in range(32):
            r = g % 2
            sl = wsl[r]
            mi0 = 2 * g
            p.emit("gpsimd", lambda e, sl=sl, g=g: e.dma_start(
                out=sl[:, 0:4096], in_=ins["wg"][g]),
                waits=[slab_free[r]], inc="WL", inc_by=16)
            sr = p.emit("gpsimd", lambda e, sl=sl, g=g: e.dma_start(
                out=sl[:, 4096:8192], in_=ins["wu"][g]),
                inc="WL", inc_by=16)
            lastmm = None
            b0, b1 = 2 * r, 2 * r + 1
            for kt in range(DT):
                for sub in range(2):
                    bk = (b0, b1)[sub]
                    p.emit("tensor", lambda e, sl=sl, kt=kt, sub=sub, bk=bk, fb=fb: e.matmul(
                        bank[bk][:, 0:256],
                        sl[:, kt * 256 + sub * 128: kt * 256 + sub * 128 + 128],
                        ln_bf[:, kt * TB + fb * 256: kt * TB + fb * 256 + 256],
                        start=(kt == 0), stop=False),
                        waits=[sr, ln_tok, bank_free[bk]], inc="PE")
                    lastmm = p.emit("tensor", lambda e, sl=sl, kt=kt, sub=sub, bk=bk, fb=fb: e.matmul(
                        bank[bk][:, 256:512],
                        sl[:, 4096 + kt * 256 + sub * 128: 4096 + kt * 256 + sub * 128 + 128],
                        ln_bf[:, kt * TB + fb * 256: kt * TB + fb * 256 + 256],
                        start=False, stop=(kt == DT - 1)),
                        inc="PE")
            slab_free[r] = lastmm
            slab_gate = lastmm
            for sub in range(2):
                bk = (b0, b1)[sub]
                mi = mi0 + sub
                gt = p.emit("scalar", lambda e, bk=bk, mi=mi: e.activation(
                    g_bf[:, :], bank[bk][:, 0:256], AF.Silu,
                    scale=sg_c[:, mi:mi + 1]), waits=[lastmm, prev_ft], inc="SE")
                ut = p.emit("vector", lambda e, bk=bk, mi=mi: e.tensor_scalar_mul(
                    u_bf[:, :], bank[bk][:, 256:512], su_c[:, mi:mi + 1]),
                    waits=[lastmm, gt], inc="VE")
                ft = p.emit("vector", lambda e, mi=mi: e.tensor_mul(
                    ubuf[:, mi * 256:(mi + 1) * 256], g_bf[:, :], u_bf[:, :]),
                    waits=[gt], inc="VE")
                bank_free[bk] = ft
                prev_ft = ft

        slab_free = [slab_gate, slab_gate]
        for kq in range(16):
            r = kq % 2
            sl = wsl[r]
            sr = p.emit("gpsimd", lambda e, sl=sl, kq=kq: e.dma_start(
                out=sl[:, 0:8192], in_=ins["wd"][kq]),
                waits=[slab_free[r]], inc="WL", inc_by=16)
            lastmm = None
            for k4 in range(4):
                k = kq * 4 + k4
                for mo in range(16):
                    bk, half = mo // 2, mo % 2
                    lastmm = p.emit("tensor", lambda e, sl=sl, k4=k4, mo=mo, k=k, bk=bk, half=half: e.matmul(
                        bank[bk][:, half * 256:(half + 1) * 256],
                        sl[:, k4 * 2048 + mo * 128: k4 * 2048 + (mo + 1) * 128],
                        ubuf[:, k * 256:(k + 1) * 256],
                        start=(k == 0 and half == 0),
                        stop=(k == 63 and half == 1)),
                        waits=[sr, prev_ft if k == 0 else None], inc="PE")
            slab_free[r] = lastmm
            slab_gate = lastmm
        dn_done = p.tok("PE")
        for mo in range(16):
            bk, half = mo // 2, mo % 2
            p.emit("vector", lambda e, mo=mo, bk=bk, half=half, fb=fb: e.scalar_tensor_tensor(
                xblk[:, mo * TB + fb * 256: mo * TB + fb * 256 + 256],
                bank[bk][:, half * 256:(half + 1) * 256], sd_c[:, mo:mo + 1],
                xblk[:, mo * TB + fb * 256: mo * TB + fb * 256 + 256],
                ALU.mult, ALU.add), waits=[dn_done], inc="VE")
        prev_p2 = p.tok("VE")
    p.barrier()
    return p.tok("VE")



def build_l1():
    nc = bass.Bass("TRN2", target_bir_lowering=False, debug=False)
    x_ap = nc.dram_tensor("x", [D, CHUNK], F32, kind="ExternalInput").ap()
    wkv_ap = nc.dram_tensor("wkv", [32, 128, 2048], F32, kind="ExternalInput").ap()
    sums_ap = nc.dram_tensor("sums", [128, 32], F32, kind="ExternalOutput").ap()

    with ExitStack() as stack:
        bu = alloc_common(nc, stack, "l1")
        e = stack.enter_context
        bu["ks"] = e(nc.sbuf_tensor("ks", [128, 32], F32))
        p = Prog(nc)
        emit_consts(p, bu)
        p.emit("vector", lambda e2: e2.memset(bu["ks"][:, :], 0.0), inc="VE")

        xblk, ln_bf, tsc, wsl, bank = (
            bu["xblk"], bu["ln_bf"], bu["tsc"], bu["wsl"], bu["bank"])
        xsrc = x_ap.rearrange("(dt p) t -> p dt t", p=128)
        tmin = tsc[:, 0:512]
        texp = tsc[:, 512:1024]
        phik = tsc[:, 1024:1536]
        vvv = tsc[:, 1536:2048]
        colk = tsc[:, 2048:2049]
        colv = tsc[:, 2049:2050]

        for blk in range(NBLK):
            xld = p.emit("sync", lambda e2, blk=blk: e2.dma_start(
                out=xblk[:, :].rearrange("p (dt t) -> p dt t", t=TB),
                in_=xsrc[:, :, blk * TB:(blk + 1) * TB]),
                waits=[p.tok("VE")], inc="XL", inc_by=16)
            ln_tok = emit_ln(p, bu, xld)
            p.barrier()

            slab_free = [None, None]
            bank_gate = [None, None]
            prev_phi = None
            for dt in range(DT):
                r = dt % 2
                sl = wsl[r]
                p.emit("gpsimd", lambda e2, sl=sl, dt=dt: e2.dma_start(
                    out=sl[:, 0:2048], in_=wkv_ap[dt]),
                    waits=[slab_free[r]], inc="WL", inc_by=16)
                sr = p.emit("gpsimd", lambda e2, sl=sl, dt=dt: e2.dma_start(
                    out=sl[:, 2048:4096], in_=wkv_ap[16 + dt]),
                    inc="WL", inc_by=16)
                lastmm2 = None
                for ch in range(2):
                    pi = (2 * dt + ch) % 2
                    b0 = 2 * pi
                    gate = bank_gate[pi]
                    lastmm = None
                    for kt in range(DT):
                        p.emit("tensor", lambda e2, sl=sl, kt=kt, ch=ch, b0=b0: e2.matmul(
                            bank[b0][:, :], sl[:, kt * 128:(kt + 1) * 128],
                            ln_bf[:, kt * TB + ch * 512: kt * TB + ch * 512 + 512],
                            start=(kt == 0), stop=(kt == DT - 1)),
                            waits=[sr, ln_tok, gate if kt == 0 else None],
                            inc="PE")
                        lastmm = p.emit("tensor", lambda e2, sl=sl, kt=kt, ch=ch, b0=b0: e2.matmul(
                            bank[b0 + 1][:, :],
                            sl[:, 2048 + kt * 128: 2048 + (kt + 1) * 128],
                            ln_bf[:, kt * TB + ch * 512: kt * TB + ch * 512 + 512],
                            start=(kt == 0), stop=(kt == DT - 1)),
                            inc="PE")
                    lastmm2 = lastmm
                    t1 = p.emit("vector", lambda e2, b0=b0: e2.tensor_scalar_min(
                        tmin, bank[b0][:, :], 0.0),
                        waits=[lastmm, prev_phi], inc="VE")
                    t2 = p.emit("scalar", lambda e2: e2.activation(
                        texp, tmin, AF.Exp), waits=[t1, prev_phi], inc="SE")
                    t3 = p.emit("vector", lambda e2, b0=b0: e2.scalar_tensor_tensor(
                        phik, bank[b0][:, :], 0.0, texp, ALU.max, ALU.add),
                        waits=[t2], inc="VE")
                    prev_phi = t3
                    t4 = p.emit("vector", lambda e2, b0=b0: e2.tensor_copy(
                        vvv, bank[b0 + 1][:, :]), waits=[lastmm], inc="VE")
                    t5 = p.emit("vector", lambda e2: e2.reduce_sum(
                        colk, phik, axis=AX.X), inc="VE")
                    p.emit("vector", lambda e2: e2.tensor_mul(vvv, phik, vvv), inc="VE")
                    t6 = p.emit("vector", lambda e2: e2.reduce_sum(
                        colv, vvv, axis=AX.X), inc="VE")
                    p.emit("vector", lambda e2, dt=dt: e2.tensor_add(
                        bu["ks"][:, dt:dt + 1], bu["ks"][:, dt:dt + 1], colk), inc="VE")
                    bank_gate[pi] = p.emit("vector", lambda e2, dt=dt: e2.tensor_add(
                        bu["ks"][:, 16 + dt:17 + dt], bu["ks"][:, 16 + dt:17 + dt],
                        colv), inc="VE")
                slab_free[r] = lastmm2
            p.barrier()

        st = p.emit("scalar", lambda e2: e2.dma_start(
            out=sums_ap, in_=bu["ks"][:, :]),
            waits=[p.tok("VE")], inc="ST", inc_by=16)
        p.emit("sync", None, waits=[st])
        p.finalize(stack)
    return nc



def build_l2():
    nc = bass.Bass("TRN2", target_bir_lowering=False, debug=False)
    ins = {}
    for nm, shape, dt_ in (
            ("x", [D, CHUNK], F32), ("wqkv", [48, 128, 2048], F32),
            ("wout", [16, 128, 2048], F32), ("wg", [32, 128, 4096], I8),
            ("wu", [32, 128, 4096], I8), ("wd", [16, 128, 8192], I8),
            ("sg", [128, 64], F32), ("su", [128, 64], F32), ("sd", [128, 16], F32),
            ("xp", [128, DT * 96], F32), ("dtp", [DTR, D], F32), ("dtb", [128, 16], F32),
            ("alog", [128, 256], F32), ("dv", [128, 16], F32),
            ("koff", [3, 128, 32], F32)):
        ins[nm] = nc.dram_tensor(nm, shape, dt_, kind="ExternalInput").ap()
    outs = {}
    for nm, shape, dt_ in (("xloc", [D, CHUNK], F32), ("scum", [D, CHUNK], F32),
                           ("cexp", [32, CHUNK], BF16), ("eexp", [128, 256], F32),
                           ("dsum", [128, 16], F32)):
        outs[nm] = nc.dram_tensor(nm, shape, dt_, kind="ExternalOutput").ap()

    with ExitStack() as stack:
        bu = alloc_common(nc, stack, "l2")
        e = stack.enter_context
        for nm, sh, dt_ in (("sg_c", [128, 64], F32), ("su_c", [128, 64], F32),
                            ("sd_c", [128, 16], F32), ("dtb_c", [128, 16], F32),
                            ("dv_c", [128, 16], F32), ("napos", [128, 256], F32),
                            ("kvcar", [128, 32], F32), ("scar", [128, 16], F32),
                            ("stcar", [128, 256], F32), ("koi", [128, 96], F32),
                            ("xps", [128, DT * 96], BF16), ("dtps", [64, D], BF16),
                            ("xdl", [64, TB], BF16), ("bcr", [32, TB], BF16),
                            ("brow", [1, 2 * TB], BF16)):
            bu[nm] = e(nc.sbuf_tensor(nm, sh, dt_))

        p = Prog(nc)
        emit_consts(p, bu)
        for nm in ("stcar", "scar"):
            p.emit("vector", lambda e2, nm=nm: e2.memset(bu[nm][:, :], 0.0), inc="VE")

        for src, dst in (("sg", "sg_c"), ("su", "su_c"), ("sd", "sd_c"),
                         ("dtb", "dtb_c"), ("dv", "dv_c"), ("alog", "napos")):
            p.emit("sync", lambda e2, src=src, dst=dst: e2.dma_start(
                out=bu[dst][:, :], in_=ins[src]), inc="XL", inc_by=16)
        p.emit("sync", lambda e2: e2.dma_start(
            out=bu["koi"][:, :].rearrange("p (s c) -> p s c", c=32),
            in_=ins["koff"].rearrange("s p c -> p s c")), inc="XL", inc_by=16)
        p.emit("gpsimd", lambda e2: e2.dma_start(
            out=bu["xps"][:, :], in_=ins["xp"]), inc="WL", inc_by=16)
        p.emit("gpsimd", lambda e2: e2.dma_start(
            out=bu["dtps"][:, :], in_=ins["dtp"]), inc="WL", inc_by=16)
        ld = p.tok("XL")
        s1 = p.emit("scalar", lambda e2: e2.activation(
            bu["napos"][:, :], bu["napos"][:, :], AF.Exp), waits=[ld], inc="SE")
        p.emit("vector", lambda e2: e2.tensor_scalar_mul(
            bu["napos"][:, :], bu["napos"][:, :], -1.0), waits=[s1], inc="VE")
        p.emit("vector", lambda e2: e2.tensor_add(
            bu["kvcar"][:, :], bu["koi"][:, 0:32], bu["koi"][:, 32:64]),
            waits=[ld], inc="VE")
        p.emit("vector", lambda e2: e2.tensor_add(
            bu["kvcar"][:, :], bu["kvcar"][:, :], bu["koi"][:, 64:96]), inc="VE")
        p.barrier()

        xblk, ln_bf, ubuf, tsc, tbf = (
            bu["xblk"], bu["ln_bf"], bu["ubuf"], bu["tsc"], bu["tbf"])
        wsl, bank = bu["wsl"], bu["bank"]
        xsrc = ins["x"].rearrange("(dt p) t -> p dt t", p=128)
        xloc_dst = outs["xloc"].rearrange("(dt p) t -> p dt t", p=128)

        phiq = tsc[:, 0:512]
        phik = tsc[:, 512:1024]
        vkv = tsc[:, 1024:1536]
        kcd = tsc[:, 1536:2048]
        kvn = tsc[:, 2048:2560]
        tmin = tbf[:, 4 * TB:4 * TB + 512]
        texp = tbf[:, 4 * TB + 512:5 * TB]

        prev_export = None
        for blk in range(NBLK):
            xld = p.emit("sync", lambda e2, blk=blk: e2.dma_start(
                out=xblk[:, :].rearrange("p (dt t) -> p dt t", t=TB),
                in_=xsrc[:, :, blk * TB:(blk + 1) * TB]),
                waits=[prev_export, p.tok("VE")], inc="XL", inc_by=16)

            ln_tok = emit_ln(p, bu, xld)
            p.barrier()

            slab_free = [None, None]
            bank_gate = [None, None]
            prev_chain = None
            for dt in range(DT):
                r = dt % 2
                sl = wsl[r]
                sr = None
                for j in range(3):
                    sr = p.emit("gpsimd", lambda e2, sl=sl, j=j, dt=dt: e2.dma_start(
                        out=sl[:, j * 2048:(j + 1) * 2048],
                        in_=ins["wqkv"][j * 16 + dt]),
                        waits=[slab_free[r]] if j == 0 else (),
                        inc="WL", inc_by=16)
                lastmm2 = None
                for ch in range(2):
                    pi = (2 * dt + ch) % 2
                    b0 = 3 * pi
                    gate = bank_gate[pi]
                    lastmm = None
                    for j in range(3):
                        for kt in range(DT):
                            lastmm = p.emit("tensor", lambda e2, sl=sl, j=j, ch=ch, kt=kt, b0=b0: e2.matmul(
                                bank[b0 + j][:, :],
                                sl[:, j * 2048 + kt * 128: j * 2048 + (kt + 1) * 128],
                                ln_bf[:, kt * TB + ch * 512: kt * TB + ch * 512 + 512],
                                start=(kt == 0), stop=(kt == DT - 1)),
                                waits=[sr, ln_tok, gate if kt == 0 else None],
                                inc="PE")
                    lastmm2 = lastmm
                    for j, dst in ((0, phiq), (1, phik)):
                        t1 = p.emit("vector", lambda e2, j=j, b0=b0: e2.tensor_scalar_min(
                            tmin, bank[b0 + j][:, :], 0.0),
                            waits=[lastmm, prev_chain], inc="VE")
                        t2 = p.emit("scalar", lambda e2: e2.activation(
                            texp, tmin, AF.Exp), waits=[t1, prev_chain], inc="SE")
                        t3 = p.emit("vector", lambda e2, j=j, b0=b0, dst=dst: e2.scalar_tensor_tensor(
                            dst, bank[b0 + j][:, :], 0.0, texp,
                            ALU.max, ALU.add), waits=[t2], inc="VE")
                        prev_chain = t3
                    p.emit("vector", lambda e2, b0=b0: e2.tensor_copy(
                        vkv, bank[b0 + 2][:, :]), waits=[lastmm], inc="VE")
                    p.emit("vector", lambda e2: e2.tensor_mul(vkv, phik, vkv), inc="VE")
                    sck = p.emit("vector", lambda e2, dt=dt: e2.tensor_tensor_scan(
                        kcd, bu["onest"][:, 0:512], phik,
                        bu["kvcar"][:, dt:dt + 1], ALU.mult, ALU.add), inc="V2")
                    sckv = p.emit("vector", lambda e2, dt=dt: e2.tensor_tensor_scan(
                        kvn, bu["onest"][:, 0:512], vkv,
                        bu["kvcar"][:, 16 + dt:17 + dt], ALU.mult, ALU.add), inc="V2")
                    p.emit("vector", lambda e2, dt=dt: e2.tensor_copy(
                        bu["kvcar"][:, dt:dt + 1], kcd[:, 511:512]),
                        waits=[sck], inc="VE")
                    p.emit("vector", lambda e2, dt=dt: e2.tensor_copy(
                        bu["kvcar"][:, 16 + dt:17 + dt], kvn[:, 511:512]),
                        waits=[sckv], inc="VE")
                    p.emit("vector", lambda e2: e2.tensor_mul(kvn, phiq, kvn),
                        waits=[sckv], inc="VE")
                    p.emit("vector", lambda e2: e2.tensor_mul(kcd, phiq, kcd),
                        waits=[sck], inc="VE")
                    p.emit("vector", lambda e2: e2.tensor_scalar_add(
                        kcd, kcd, A_EPS), inc="VE")
                    p.emit("vector", lambda e2: e2.reciprocal(kcd, kcd), inc="VE")
                    prev_chain = p.emit("vector", lambda e2, dt=dt, ch=ch: e2.tensor_mul(
                        ubuf[:, dt * 1024 + ch * 512: dt * 1024 + ch * 512 + 512],
                        kvn, kcd), inc="VE")
                    bank_gate[pi] = prev_chain
                slab_free[r] = lastmm2
            attn_ev = p.tok("VE")
            p.barrier()

            slab_free = [None, None]
            prev_ev = [None] * 4
            for m in range(DT):
                r = m % 2
                sl = wsl[r]
                sr = p.emit("gpsimd", lambda e2, sl=sl, m=m: e2.dma_start(
                    out=sl[:, 0:2048], in_=ins["wout"][m]),
                    waits=[slab_free[r]], inc="WL", inc_by=16)
                lastmm = None
                for ch in range(2):
                    bk = 2 * r + ch
                    for kt in range(DT):
                        lastmm = p.emit("tensor", lambda e2, sl=sl, ch=ch, kt=kt, bk=bk: e2.matmul(
                            bank[bk][:, :], sl[:, kt * 128:(kt + 1) * 128],
                            ubuf[:, kt * 1024 + ch * 512: kt * 1024 + ch * 512 + 512],
                            start=(kt == 0), stop=(kt == DT - 1)),
                            waits=[sr, attn_ev, prev_ev[bk]], inc="PE")
                slab_free[r] = lastmm
                for ch in range(2):
                    bk = 2 * r + ch
                    prev_ev[bk] = p.emit("vector", lambda e2, m=m, ch=ch, bk=bk: e2.scalar_tensor_tensor(
                        xblk[:, m * TB + ch * 512: m * TB + ch * 512 + 512],
                        bank[bk][:, :], HD_SCALE,
                        xblk[:, m * TB + ch * 512: m * TB + ch * 512 + 512],
                        ALU.mult, ALU.add), waits=[lastmm], inc="VE")
            p.barrier()

            ln_tok = emit_ln(p, bu, None)
            p.barrier()
            emit_ffn(p, bu, ins, ln_tok)

            ln_tok = emit_ln(p, bu, None)
            p.barrier()

            lastmm = None
            for ch in range(2):
                for kt in range(DT):
                    lastmm = p.emit("tensor", lambda e2, ch=ch, kt=kt: e2.matmul(
                        bank[6 + ch][0:96, :], bu["xps"][:, kt * 96:(kt + 1) * 96],
                        ln_bf[:, kt * TB + ch * 512: kt * TB + ch * 512 + 512],
                        start=(kt == 0), stop=(kt == DT - 1)),
                        waits=[ln_tok], inc="PE")
            for ch in range(2):
                p.emit("vector", lambda e2, ch=ch: e2.tensor_copy(
                    bu["xdl"][:, ch * 512:(ch + 1) * 512], bank[6 + ch][0:64, :]),
                    waits=[lastmm], inc="VE")
                p.emit("vector", lambda e2, ch=ch: e2.tensor_copy(
                    bu["bcr"][:, ch * 512:(ch + 1) * 512], bank[6 + ch][64:96, :]),
                    waits=[lastmm], inc="VE")
            xdl_tok = p.tok("VE")
            bc_exp = p.emit("scalar", lambda e2, blk=blk: e2.dma_start(
                out=outs["cexp"][:, blk * TB:(blk + 1) * TB],
                in_=bu["bcr"][0:32, :]), waits=[xdl_tok], inc="ST", inc_by=16)
            p.barrier()

            zt = tsc[:, 0:TB]
            pt = tsc[:, TB:2 * TB]
            scm = tsc[:, 2048:3072]
            scm_exp = None
            for m in range(DT):
                mm1 = p.emit("tensor", lambda e2, m=m: e2.matmul(
                    bank[m % 2][:, :], bu["dtps"][0:64, m * 128:(m + 1) * 128],
                    bu["xdl"][:, 0:512], start=True, stop=True),
                    waits=[xdl_tok, p.tok("VE") if m >= 2 else None], inc="PE")
                mm2 = p.emit("tensor", lambda e2, m=m: e2.matmul(
                    bank[2 + m % 2][:, :], bu["dtps"][0:64, m * 128:(m + 1) * 128],
                    bu["xdl"][:, 512:1024], start=True, stop=True), inc="PE")
                p.emit("vector", lambda e2, m=m: e2.tensor_scalar(
                    zt[:, 0:512], bank[m % 2][:, :], bu["dtb_c"][:, m:m + 1], 30.0,
                    ALU.add, ALU.min), waits=[mm2], inc="VE")
                t1 = p.emit("vector", lambda e2, m=m: e2.tensor_scalar(
                    zt[:, 512:1024], bank[2 + m % 2][:, :], bu["dtb_c"][:, m:m + 1],
                    30.0, ALU.add, ALU.min), inc="VE")
                t2 = p.emit("scalar", lambda e2: e2.activation(
                    pt, zt, AF.Exp), waits=[t1], inc="SE")
                t3 = p.emit("scalar", lambda e2, m=m: e2.activation(
                    ubuf[:, m * TB:(m + 1) * TB], pt, AF.Ln, bias=bu["cst"][:, 0:1]),
                    waits=[t2], inc="SE")
                p.emit("vector", lambda e2, m=m: e2.scalar_tensor_tensor(
                    xblk[:, m * TB:(m + 1) * TB], ln_bf[:, m * TB:(m + 1) * TB],
                    bu["dv_c"][:, m:m + 1], xblk[:, m * TB:(m + 1) * TB],
                    ALU.mult, ALU.add), waits=[t3], inc="VE")
                sc = p.emit("vector", lambda e2, m=m: e2.tensor_tensor_scan(
                    scm, bu["onest"][:, :], ubuf[:, m * TB:(m + 1) * TB],
                    bu["scar"][:, m:m + 1], ALU.mult, ALU.add),
                    waits=[scm_exp], inc="V2")
                p.emit("vector", lambda e2, m=m: e2.tensor_copy(
                    bu["scar"][:, m:m + 1], scm[:, TB - 1:TB]),
                    waits=[sc], inc="VE")
                scm_exp = p.emit("scalar", lambda e2, m=m, blk=blk: e2.dma_start(
                    out=outs["scum"][m * 128:(m + 1) * 128, blk * TB:(blk + 1) * TB],
                    in_=scm[:, :]), waits=[sc], inc="ST", inc_by=16)
                p.emit("vector", lambda e2, m=m: e2.tensor_mul(
                    ln_bf[:, m * TB:(m + 1) * TB], ubuf[:, m * TB:(m + 1) * TB],
                    ln_bf[:, m * TB:(m + 1) * TB]), inc="VE")
            p.barrier()

            dA = [tbf[:, 0:TB], tbf[:, TB:2 * TB]]
            dBu = tbf[:, 2 * TB:3 * TB]
            stt_ = tbf[:, 3 * TB:4 * TB]
            tmpy = tbf[:, 4 * TB:5 * TB]
            bbc, cbc = bu["rbc"], bu["nmrbc"]
            prev_scan = [None, None]
            prev_rd = None
            for n in range(NST):
                bl = p.emit("sync", lambda e2, n=n, blk=blk: e2.dma_start(
                    out=bbc[:, :],
                    in_=outs["cexp"][n:n + 1,
                                     blk * TB:(blk + 1) * TB].broadcast_to([128, TB])),
                    waits=[bc_exp, prev_rd], inc="XL", inc_by=16)
                cl = p.emit("sync", lambda e2, n=n, blk=blk: e2.dma_start(
                    out=cbc[:, :],
                    in_=outs["cexp"][16 + n:17 + n,
                                     blk * TB:(blk + 1) * TB].broadcast_to([128, TB])),
                    inc="XL", inc_by=16)
                bc_done = cl
                for dt in range(DT):
                    col = dt * 16 + n
                    r = dt % 2
                    sa = p.emit("scalar", lambda e2, dt=dt, col=col, r=r: e2.activation(
                        dA[r], ubuf[:, dt * TB:(dt + 1) * TB], AF.Exp,
                        scale=bu["napos"][:, col:col + 1]),
                        waits=[prev_scan[r]], inc="SE")
                    p.emit("vector", lambda e2, dt=dt: e2.tensor_mul(
                        dBu, ln_bf[:, dt * TB:(dt + 1) * TB], bbc[:, :]),
                        waits=[bc_done], inc="VE")
                    sc = p.emit("vector", lambda e2, col=col, r=r: e2.tensor_tensor_scan(
                        stt_, dA[r], dBu, bu["stcar"][:, col:col + 1],
                        ALU.mult, ALU.add), waits=[sa], inc="V2")
                    prev_scan[r] = sc
                    p.emit("vector", lambda e2, col=col: e2.tensor_copy(
                        bu["stcar"][:, col:col + 1], stt_[:, TB - 1:TB]),
                        waits=[sc], inc="VE")
                    prev_rd = p.emit("vector", lambda e2: e2.tensor_mul(
                        tmpy, stt_, cbc[:, :]), waits=[sc], inc="VE")
                    p.emit("vector", lambda e2, dt=dt: e2.tensor_add(
                        xblk[:, dt * TB:(dt + 1) * TB],
                        xblk[:, dt * TB:(dt + 1) * TB], tmpy), inc="VE")
            p.barrier()

            prev_export = p.emit("scalar", lambda e2, blk=blk: e2.dma_start(
                out=xloc_dst[:, :, blk * TB:(blk + 1) * TB],
                in_=xblk[:, :].rearrange("p (dt t) -> p dt t", t=TB)),
                waits=[p.tok("VE")], inc="ST", inc_by=16)

        p.emit("scalar", lambda e2: e2.dma_start(
            out=outs["eexp"], in_=bu["stcar"][:, :]),
            waits=[p.tok("VE")], inc="ST", inc_by=16)
        p.emit("scalar", lambda e2: e2.dma_start(
            out=outs["dsum"], in_=bu["scar"][:, :]), inc="ST", inc_by=16)
        p.emit("sync", None, waits=[p.tok("ST")])
        p.finalize(stack)
    return nc



def build_l3():
    nc = bass.Bass("TRN2", target_bir_lowering=False, debug=False)
    ins = {}
    for nm, shape, dt_ in (
            ("xloc", [D, CHUNK], F32), ("scum", [D, CHUNK], F32),
            ("cexp", [32, CHUNK], BF16), ("epred", [3, 128, 256], F32),
            ("dsr", [3, 128, 256], F32), ("alog", [128, 256], F32),
            ("wg", [32, 128, 4096], I8), ("wu", [32, 128, 4096], I8),
            ("wd", [16, 128, 8192], I8),
            ("sg", [128, 64], F32), ("su", [128, 64], F32), ("sd", [128, 16], F32)):
        ins[nm] = nc.dram_tensor(nm, shape, dt_, kind="ExternalInput").ap()
    y_ap = nc.dram_tensor("y", [D, CHUNK], F32, kind="ExternalOutput").ap()

    with ExitStack() as stack:
        bu = alloc_common(nc, stack, "l3")
        e = stack.enter_context
        for nm, sh in (("sg_c", [128, 64]), ("su_c", [128, 64]), ("sd_c", [128, 16]),
                       ("napos", [128, 256]), ("s0", [128, 256]),
                       ("eds", [128, 6 * 256])):
            bu[nm] = e(nc.sbuf_tensor(nm, sh, F32))
        bu["brow"] = e(nc.sbuf_tensor("brow", [1, TB], BF16))

        p = Prog(nc)
        emit_consts(p, bu)
        for src, dst in (("sg", "sg_c"), ("su", "su_c"), ("sd", "sd_c"),
                         ("alog", "napos")):
            p.emit("sync", lambda e2, src=src, dst=dst: e2.dma_start(
                out=bu[dst][:, :], in_=ins[src]), inc="XL", inc_by=16)
        for s in range(3):
            p.emit("sync", lambda e2, s=s: e2.dma_start(
                out=bu["eds"][:, s * 256:(s + 1) * 256], in_=ins["epred"][s]),
                inc="XL", inc_by=16)
            p.emit("sync", lambda e2, s=s: e2.dma_start(
                out=bu["eds"][:, (3 + s) * 256:(4 + s) * 256], in_=ins["dsr"][s]),
                inc="XL", inc_by=16)
        ld = p.tok("XL")
        s1 = p.emit("scalar", lambda e2: e2.activation(
            bu["napos"][:, :], bu["napos"][:, :], AF.Exp), waits=[ld], inc="SE")
        p.emit("vector", lambda e2: e2.tensor_scalar_mul(
            bu["napos"][:, :], bu["napos"][:, :], -1.0), waits=[s1], inc="VE")
        p.emit("vector", lambda e2: e2.memset(bu["s0"][:, :], 0.0), inc="VE")
        tA = bu["tsc"][:, 0:256]
        for s in range(3):
            v1 = p.emit("vector", lambda e2, s=s: e2.tensor_mul(
                tA, bu["napos"][:, :], bu["eds"][:, (3 + s) * 256:(4 + s) * 256]),
                waits=[ld], inc="VE")
            s2 = p.emit("scalar", lambda e2: e2.activation(
                tA, tA, AF.Exp), waits=[v1], inc="SE")
            p.emit("vector", lambda e2: e2.tensor_mul(
                bu["s0"][:, :], bu["s0"][:, :], tA), waits=[s2], inc="VE")
            p.emit("vector", lambda e2, s=s: e2.tensor_add(
                bu["s0"][:, :], bu["s0"][:, :], bu["eds"][:, s * 256:(s + 1) * 256]),
                inc="VE")
        p.barrier()

        xblk, ln_bf, ubuf, tsc, tbf = (
            bu["xblk"], bu["ln_bf"], bu["ubuf"], bu["tsc"], bu["tbf"])
        bank = bu["bank"]
        xsrc = ins["xloc"].rearrange("(dt p) t -> p dt t", p=128)
        y_dst = y_ap.rearrange("(dt p) t -> p dt t", p=128)

        prev_export = None
        for blk in range(NBLK):
            xld = p.emit("sync", lambda e2, blk=blk: e2.dma_start(
                out=xblk[:, :].rearrange("p (dt t) -> p dt t", t=TB),
                in_=xsrc[:, :, blk * TB:(blk + 1) * TB]),
                waits=[prev_export, p.tok("VE")], inc="XL", inc_by=16)

            for n in range(NST):
                p.emit("sync", lambda e2, n=n, blk=blk: e2.dma_start(
                    out=ubuf[:, n * TB:(n + 1) * TB],
                    in_=ins["cexp"][16 + n:17 + n,
                                    blk * TB:(blk + 1) * TB].broadcast_to([128, TB])),
                    waits=[p.tok("VE") if n == 0 else None], inc="XL", inc_by=16)
            cbc_ld = p.tok("XL")
            p.barrier()

            scm = [tsc[:, 0:TB], tsc[:, TB:2 * TB]]
            cp = [tbf[:, 0:TB], tbf[:, TB:2 * TB]]
            tt = tbf[:, 2 * TB:3 * TB]
            sld = [None, None]
            prev_v1 = [None, None]
            for dt in range(DT):
                r = dt % 2
                sld[r] = p.emit("sync", lambda e2, dt=dt, blk=blk, r=r: e2.dma_start(
                    out=scm[r][:, :],
                    in_=ins["scum"][dt * 128:(dt + 1) * 128,
                                    blk * TB:(blk + 1) * TB]),
                    waits=[p.tok("VE") if dt >= 2 else None], inc="XL", inc_by=16)
                for n in range(NST):
                    col = dt * 16 + n
                    q = n % 2
                    s1 = p.emit("scalar", lambda e2, r=r, col=col, q=q: e2.activation(
                        cp[q][:, :], scm[r][:, :], AF.Exp,
                        scale=bu["napos"][:, col:col + 1]),
                        waits=[sld[r], prev_v1[q]], inc="SE")
                    v1 = p.emit("vector", lambda e2, n=n, col=col, q=q: e2.scalar_tensor_tensor(
                        tt, cp[q][:, :], bu["s0"][:, col:col + 1],
                        ubuf[:, n * TB:(n + 1) * TB], ALU.mult, ALU.mult),
                        waits=[s1], inc="VE")
                    prev_v1[q] = v1
                    p.emit("vector", lambda e2, dt=dt: e2.tensor_add(
                        xblk[:, dt * TB:(dt + 1) * TB],
                        xblk[:, dt * TB:(dt + 1) * TB], tt), inc="VE")
            p.barrier()

            ln_tok = emit_ln(p, bu, None)
            p.barrier()
            emit_ffn(p, bu, ins, ln_tok)

            prev_export = p.emit("scalar", lambda e2, blk=blk: e2.dma_start(
                out=y_dst[:, :, blk * TB:(blk + 1) * TB],
                in_=xblk[:, :].rearrange("p (dt t) -> p dt t", t=TB)),
                waits=[p.tok("VE")], inc="ST", inc_by=16)

        p.emit("sync", None, waits=[p.tok("ST")])
        p.finalize(stack)
    return nc



def _col128(v):
    return np.ascontiguousarray(np.asarray(v, np.float32).reshape(-1, 128).T)


def kernel(L0_attn_a_x, L0_attn_a_W_qkv, L0_attn_a_W_out,
           L0_attn_b_x_proj_weight, L0_attn_b_dt_proj_weight,
           L0_attn_b_dt_proj_bias, L0_attn_b_A_log, L0_attn_b_D,
           L0_ffn_w_gate, L0_ffn_w_up, L0_ffn_w_down,
           L0_ffn_s_gate, L0_ffn_s_up, L0_ffn_s_down):
    f32 = np.float32
    x = np.asarray(L0_attn_a_x, f32)
    x_cm = np.ascontiguousarray(x.transpose(0, 2, 1))
    wqkv_t = np.asarray(L0_attn_a_W_qkv, f32).T
    wqkv = np.ascontiguousarray(
        wqkv_t.reshape(16, 128, 48, 128).transpose(2, 1, 0, 3).reshape(48, 128, 2048))
    wkv = np.ascontiguousarray(wqkv[16:])
    wout = np.ascontiguousarray(
        np.asarray(L0_attn_a_W_out, f32).T
        .reshape(16, 128, 16, 128).transpose(2, 1, 0, 3).reshape(16, 128, 2048))
    wg = np.ascontiguousarray(np.asarray(L0_ffn_w_gate)
                              .reshape(16, 128, 32, 256).transpose(2, 1, 0, 3)
                              .reshape(32, 128, 4096))
    wu = np.ascontiguousarray(np.asarray(L0_ffn_w_up)
                              .reshape(16, 128, 32, 256).transpose(2, 1, 0, 3)
                              .reshape(32, 128, 4096))
    wd = np.ascontiguousarray(np.asarray(L0_ffn_w_down)
                              .reshape(16, 4, 128, 2048).transpose(0, 2, 1, 3)
                              .reshape(16, 128, 8192))
    sg_c, su_c, sd_c = _col128(L0_ffn_s_gate), _col128(L0_ffn_s_up), _col128(L0_ffn_s_down)
    dtb_c, dv_c = _col128(L0_attn_b_dt_proj_bias), _col128(L0_attn_b_D)
    alog_cm = np.ascontiguousarray(
        np.asarray(L0_attn_b_A_log, f32).reshape(16, 128, 16)
        .transpose(1, 0, 2).reshape(128, 256))
    xp = np.ascontiguousarray(np.asarray(L0_attn_b_x_proj_weight, f32)
                              .reshape(16, 128, 96).transpose(1, 0, 2).reshape(128, 1536))
    dtp = np.ascontiguousarray(np.asarray(L0_attn_b_dt_proj_weight, f32))

    xs = [np.ascontiguousarray(x_cm[c // 4][:, (c % 4) * CHUNK:(c % 4 + 1) * CHUNK])
          for c in range(NCORE)]

    nc1 = build_l1()
    res1 = run_bass_kernel_spmd(
        nc1, [{"x": xs[c], "wkv": wkv} for c in range(NCORE)],
        core_ids=list(range(NCORE)))
    sums = [np.asarray(res1.results[c]["sums"]) for c in range(NCORE)]

    nc2 = build_l2()
    maps2 = []
    for c in range(NCORE):
        bi, j = c // 4, c % 4
        koff = np.zeros((3, 128, 32), f32)
        for i in range(j):
            koff[i] = sums[bi * 4 + i]
        maps2.append({"x": xs[c], "wqkv": wqkv, "wout": wout,
                      "wg": wg, "wu": wu, "wd": wd, "sg": sg_c, "su": su_c,
                      "sd": sd_c, "xp": xp, "dtp": dtp, "dtb": dtb_c,
                      "alog": alog_cm, "dv": dv_c, "koff": koff})
    res2 = run_bass_kernel_spmd(nc2, maps2, core_ids=list(range(NCORE)))

    nc3 = build_l3()
    maps3 = []
    for c in range(NCORE):
        bi, j = c // 4, c % 4
        epred = np.zeros((3, 128, 256), f32)
        dsr = np.zeros((3, 128, 256), f32)
        for i in range(j):
            rc = res2.results[bi * 4 + i]
            epred[i] = np.asarray(rc["eexp"])
            dsr[i] = np.repeat(np.asarray(rc["dsum"]), 16, axis=1)
        rme = res2.results[c]
        maps3.append({"xloc": np.asarray(rme["xloc"]),
                      "scum": np.asarray(rme["scum"]),
                      "cexp": np.asarray(rme["cexp"]),
                      "epred": epred, "dsr": dsr, "alog": alog_cm,
                      "wg": wg, "wu": wu, "wd": wd,
                      "sg": sg_c, "su": su_c, "sd": sd_c})
    res3 = run_bass_kernel_spmd(nc3, maps3, core_ids=list(range(NCORE)))

    out = np.empty((B, L, D), f32)
    for c in range(NCORE):
        bi, j = c // 4, c % 4
        out[bi, j * CHUNK:(j + 1) * CHUNK, :] = np.asarray(res3.results[c]["y"]).T
    return out


# version 12
# speedup vs baseline: 1.4811x; 1.1963x over previous
import numpy as np
from contextlib import ExitStack

import concourse.bass as bass
import concourse.mybir as mybir
from concourse.bass_utils import run_bass_kernel_spmd

F32 = mybir.dt.float32
BF16 = mybir.dt.bfloat16
I8 = mybir.dt.int8
AF = mybir.ActivationFunctionType
ALU = mybir.AluOpType
AX = mybir.AxisListType

NCORE = 8
B, L, D = 2, 8192, 2048
CHUNK = 2048
NBLK, TB = 4, 512
DT = 16
NST = 16
DTR = 64
INTER = 8192
MI = 64
HD_SCALE = float(128 ** -0.5)
INV_D = 1.0 / 2048.0
LN_EPS = 1e-5
A_EPS = 1e-9

ENGINES = ("tensor", "vector", "scalar", "gpsimd", "sync")
INTERLEAVE = True


class Prog:

    OWN = {"tensor": "PE", "vector": "VE", "scalar": "SE"}

    def __init__(self, nc):
        self.nc = nc
        self.ops = {e: [] for e in ENGINES}
        self.counts = {}

    def tok(self, sem):
        return (sem, self.counts.get(sem, 0), 0)

    def emit(self, eng, fn, waits=(), inc=None, inc_by=1):
        tok = None
        if inc is not None:
            self.counts[inc] = self.counts.get(inc, 0) + inc_by
            tok = (inc, self.counts[inc], inc_by)
        ws = tuple((w[0], w[1]) for w in waits if w is not None)
        self.ops[eng].append((ws, fn, tok))
        return tok

    def barrier(self):
        toks = [(s, self.counts[s])
                for s in ("PE", "VE", "SE", "V2", "WL", "XL", "ST")
                if s in self.counts]
        for e in ENGINES:
            self.ops[e].append((tuple(toks), None, None))

    def finalize(self, stack):
        nc = self.nc
        sems = {name: stack.enter_context(nc.semaphore(name))
                for name in self.counts}
        block = stack.enter_context(nc.Block())

        def mkbody(ops, own):
            def body(eng):
                waited = {}
                for ws, fn, tok in ops:
                    for (s, v) in ws:
                        if s == own or waited.get(s, 0) >= v:
                            continue
                        eng.wait_ge(sems[s], v)
                        waited[s] = v
                    inst = fn(eng) if fn is not None else None
                    if tok is not None and inst is not None:
                        inst.then_inc(sems[tok[0]], tok[2])
            return body

        for engname in ENGINES:
            getattr(block, engname)(mkbody(self.ops[engname], Prog.OWN.get(engname)))



def alloc_common(nc, stack, kind):
    bu = {}
    e = stack.enter_context
    bu["xblk"] = e(nc.sbuf_tensor("xblk", [128, 2 * DT * TB], F32))
    bu["ln_bf"] = e(nc.sbuf_tensor("ln_bf", [128, 2 * DT * TB], BF16))
    if kind == "l2":
        bu["ubuf"] = e(nc.sbuf_tensor("ubuf", [128, 24576], BF16))
    elif kind == "l3":
        bu["ubuf"] = e(nc.sbuf_tensor("ubuf", [128, 32768], BF16))
    nw = 6144 if kind == "l2" else 4096
    bu["wsl"] = [e(nc.sbuf_tensor(f"wsl{i}", [128, nw], BF16)) for i in (0, 1)]
    bu["tsc"] = e(nc.sbuf_tensor("tsc", [128, 2560], F32))
    bu["tbf"] = e(nc.sbuf_tensor("tbf", [128, 4608], BF16))
    bu["cst"] = e(nc.sbuf_tensor("cst", [128, 8], F32))
    bu["cstb"] = e(nc.sbuf_tensor("cstb", [128, 2], BF16))
    bu["onesr"] = e(nc.sbuf_tensor("onesr", [1, 128], F32))
    bu["rbc"] = e(nc.sbuf_tensor("rbc", [128, TB], BF16))
    bu["nmrbc"] = e(nc.sbuf_tensor("nmrbc", [128, TB], BF16))
    if kind != "l3":
        bu["onest"] = e(nc.sbuf_tensor("onest", [128, TB], BF16))
    if kind == "l2":
        bu["bbc"] = e(nc.sbuf_tensor("bbc", [128, 2 * TB], BF16))
        bu["cbc"] = e(nc.sbuf_tensor("cbc", [128, 2 * TB], BF16))
    bu["bank"] = [e(nc.psum_tensor(f"pb{i}", [128, 512], F32)) for i in range(8)]
    return bu


def emit_consts(p, bu):
    cst, cstb = bu["cst"], bu["cstb"]
    for i, v in enumerate([1.0, LN_EPS, -0.5, 30.0, A_EPS, -1.0, 0.0, 0.0]):
        p.emit("vector", lambda e, i=i, v=v: e.memset(cst[:, i:i + 1], v), inc="VE")
    p.emit("vector", lambda e: e.memset(cstb[:, 0:1], 1.0), inc="VE")
    p.emit("vector", lambda e: e.memset(bu["onesr"][:, :], 1.0), inc="VE")
    if "onest" in bu:
        p.emit("vector", lambda e: e.memset(bu["onest"][:, :], 1.0), inc="VE")



def emit_ln(p, bu, xv, lv, xld_tok, lv_gate=None):
    tbf, cst, cstb, onesr = bu["tbf"], bu["cst"], bu["cstb"], bu["onesr"]
    rbc, nmrbc, bank = bu["rbc"], bu["nmrbc"], bu["bank"]
    srow = bu["tsc"]

    xbf = [tbf[:, 0:TB], tbf[:, TB:2 * TB]]
    sqb = [tbf[:, 2 * TB:3 * TB], tbf[:, 3 * TB:4 * TB]]

    mm_gate = [None, None]
    for dt in range(DT):
        r = dt % 2
        cp = p.emit("vector", lambda e, dt=dt, r=r: e.tensor_copy(
            xbf[r][:, :], xv[:, dt * TB:(dt + 1) * TB]),
            waits=[xld_tok, mm_gate[r]], inc="VE")
        sq = p.emit("scalar", lambda e, r=r: e.activation(
            sqb[r][:, :], xbf[r][:, :], AF.Square),
            waits=[cp, mm_gate[r]], inc="SE")
        p.emit("tensor", lambda e, r=r, dt=dt: e.matmul(
            bank[4][0:1, :], cstb[:, 0:1], xbf[r][:, :],
            start=(dt == 0), stop=(dt == DT - 1)), waits=[cp], inc="PE")
        mm_gate[r] = p.emit("tensor", lambda e, r=r, dt=dt: e.matmul(
            bank[5][0:1, :], cstb[:, 0:1], sqb[r][:, :],
            start=(dt == 0), stop=(dt == DT - 1)), waits=[sq], inc="PE")
    stats_done = p.tok("PE")

    r_mean = srow[0:1, 0:TB]
    r_rstd = srow[0:1, TB:2 * TB]
    r_nmr = srow[0:1, 2 * TB:3 * TB]
    p.emit("vector", lambda e: e.tensor_scalar_mul(
        r_mean, bank[4][0:1, :], INV_D), waits=[stats_done], inc="VE")
    p.emit("vector", lambda e: e.tensor_mul(r_nmr, r_mean, r_mean), inc="VE")
    t1 = p.emit("vector", lambda e: e.scalar_tensor_tensor(
        r_rstd, bank[5][0:1, :], INV_D, r_nmr, ALU.mult, ALU.subtract), inc="VE")
    s1 = p.emit("scalar", lambda e: e.activation(
        r_rstd, r_rstd, AF.Ln, bias=cst[0:1, 1:2]), waits=[t1], inc="SE")
    s2 = p.emit("scalar", lambda e: e.activation(
        r_rstd, r_rstd, AF.Exp, scale=cst[0:1, 2:3]), waits=[s1], inc="SE")
    nm = p.emit("vector", lambda e: e.scalar_tensor_tensor(
        r_nmr, r_mean, -1.0, r_rstd, ALU.mult, ALU.mult), waits=[s2], inc="VE")
    m1 = p.emit("tensor", lambda e: e.matmul(
        bank[6][:, :], onesr[0:1, :], r_rstd, start=True, stop=True),
        waits=[nm], inc="PE")
    m2 = p.emit("tensor", lambda e: e.matmul(
        bank[7][:, :], onesr[0:1, :], r_nmr, start=True, stop=True),
        waits=[nm], inc="PE")
    p.emit("vector", lambda e: e.tensor_copy(rbc[:, :], bank[6][:, :]),
           waits=[m1], inc="VE")
    p.emit("vector", lambda e: e.tensor_copy(nmrbc[:, :], bank[7][:, :]),
           waits=[m2], inc="VE")
    bc_ev = p.tok("VE")
    for dt in range(DT):
        r = dt % 2
        p.emit("vector", lambda e, dt=dt, r=r: e.tensor_copy(
            xbf[r][:, :], xv[:, dt * TB:(dt + 1) * TB]),
            waits=[bc_ev, lv_gate if dt == 0 else None], inc="VE")
        p.emit("vector", lambda e, r=r: e.tensor_mul(
            sqb[r][:, :], xbf[r][:, :], rbc[:, :]), inc="VE")
        p.emit("vector", lambda e, dt=dt, r=r: e.tensor_add(
            lv[:, dt * TB:(dt + 1) * TB], sqb[r][:, :], nmrbc[:, :]), inc="VE")
    return p.tok("VE")



def emit_ffn(p, bu, ins, xv, lv, ln_tok, ps, filler=None):
    ubuf, wsl, tbf, bank = bu["ubuf"], bu["wsl"], bu["tbf"], bu["bank"]
    sg_c, su_c, sd_c = bu["sg_c"], bu["su_c"], bu["sd_c"]
    g_bf = tbf[:, 0:TB]
    u_bf = tbf[:, TB:2 * TB]

    prev_ft = None
    first_ft_gate = ps.get("ratio_readers")
    for ph in range(2):
        slab_free = [ps["slab_gate"], ps["slab_gate"]]
        bank_free = [ps.get("p2_ev")] * 4
        for ml in range(MI // 2):
            mi = ph * (MI // 2) + ml
            r = ml % 2
            sl = wsl[r]
            p.emit("gpsimd", lambda e, sl=sl, mi=mi: e.dma_start(
                out=sl[:, 0:2048], in_=ins["wg"][mi]),
                waits=[slab_free[r]], inc="WL", inc_by=16)
            sr = p.emit("gpsimd", lambda e, sl=sl, mi=mi: e.dma_start(
                out=sl[:, 2048:4096], in_=ins["wu"][mi]), inc="WL", inc_by=16)
            bg = ml % 2
            bup = 2 + ml % 2
            lastmm = None
            for kt in range(DT):
                p.emit("tensor", lambda e, sl=sl, kt=kt, bg=bg: e.matmul(
                    bank[bg][:, :], sl[:, kt * 128:(kt + 1) * 128],
                    lv[:, kt * TB:(kt + 1) * TB],
                    start=(kt == 0), stop=(kt == DT - 1)),
                    waits=[sr, ln_tok, bank_free[bg] if kt == 0 else None,
                           bank_free[bup] if kt == 0 else None], inc="PE")
                lastmm = p.emit("tensor", lambda e, sl=sl, kt=kt, bup=bup: e.matmul(
                    bank[bup][:, :],
                    sl[:, 2048 + kt * 128: 2048 + (kt + 1) * 128],
                    lv[:, kt * TB:(kt + 1) * TB],
                    start=(kt == 0), stop=(kt == DT - 1)), inc="PE")
            ps["slab_gate"] = lastmm
            slab_free[r] = lastmm
            gt = p.emit("scalar", lambda e, bg=bg, mi=mi: e.activation(
                g_bf[:, :], bank[bg][:, :], AF.Silu,
                scale=sg_c[:, mi:mi + 1]), waits=[lastmm, prev_ft], inc="SE")
            ut = p.emit("vector", lambda e, bup=bup, mi=mi: e.tensor_scalar_mul(
                u_bf[:, :], bank[bup][:, :], su_c[:, mi:mi + 1]),
                waits=[lastmm, gt], inc="VE")
            ft = p.emit("vector", lambda e, ml=ml: e.tensor_mul(
                ubuf[:, ml * TB:(ml + 1) * TB], g_bf[:, :], u_bf[:, :]),
                waits=[gt, first_ft_gate], inc="VE")
            first_ft_gate = None
            bank_free[bg] = ft
            bank_free[bup] = ft
            prev_ft = ft
            if filler is not None:
                filler()
                filler()
                filler()

        for fb in range(2):
            slab_free = [ps["slab_gate"], ps["slab_gate"]]
            for kl in range(16):
                kq = ph * 16 + kl
                r = (fb * 16 + kl) % 2
                sl = wsl[r]
                sr = p.emit("gpsimd", lambda e, sl=sl, kq=kq: e.dma_start(
                    out=sl[:, 0:4096], in_=ins["wd"][kq]),
                    waits=[slab_free[r]], inc="WL", inc_by=16)
                lastmm = None
                for k2 in range(2):
                    kl2 = kl * 2 + k2
                    for mo in range(16):
                        bk, half = mo // 2, mo % 2
                        lastmm = p.emit("tensor", lambda e, sl=sl, k2=k2, mo=mo, kl2=kl2, bk=bk, half=half, fb=fb: e.matmul(
                            bank[bk][:, half * 256:(half + 1) * 256],
                            sl[:, k2 * 2048 + mo * 128: k2 * 2048 + (mo + 1) * 128],
                            ubuf[:, kl2 * TB + fb * 256: kl2 * TB + fb * 256 + 256],
                            start=(kl2 == 0 and half == 0),
                            stop=(kl2 == 31 and half == 1)),
                            waits=[sr, prev_ft if kl2 == 0 else None], inc="PE")
                ps["slab_gate"] = lastmm
                slab_free[r] = lastmm
                if filler is not None:
                    filler()
            dn_done = p.tok("PE")
            for mo in range(16):
                bk, half = mo // 2, mo % 2
                p.emit("vector", lambda e, mo=mo, bk=bk, half=half, fb=fb: e.scalar_tensor_tensor(
                    xv[:, mo * TB + fb * 256: mo * TB + fb * 256 + 256],
                    bank[bk][:, half * 256:(half + 1) * 256], sd_c[:, mo:mo + 1],
                    xv[:, mo * TB + fb * 256: mo * TB + fb * 256 + 256],
                    ALU.mult, ALU.add), waits=[dn_done], inc="VE")
            ps["p2_ev"] = p.tok("VE")
            prev_ft = ps["p2_ev"]
    return p.tok("VE")



def build_l1():
    nc = bass.Bass("TRN2", target_bir_lowering=False, debug=False)
    x_ap = nc.dram_tensor("x", [D, CHUNK], F32, kind="ExternalInput").ap()
    wkv_ap = nc.dram_tensor("wkv", [32, 128, 2048], F32, kind="ExternalInput").ap()
    sums_ap = nc.dram_tensor("sums", [128, 32], F32, kind="ExternalOutput").ap()
    import os
    lnx_ap = (nc.dram_tensor("lnx", [D, CHUNK], F32, kind="ExternalInput").ap()
              if os.environ.get("L1_LN_OVERRIDE") else None)

    with ExitStack() as stack:
        bu = alloc_common(nc, stack, "l1")
        e = stack.enter_context
        bu["ks"] = e(nc.sbuf_tensor("ks", [128, 32], F32))
        p = Prog(nc)
        emit_consts(p, bu)
        p.emit("vector", lambda e2: e2.memset(bu["ks"][:, :], 0.0), inc="VE")

        xblk, ln_bf, tsc, tbf, wsl, bank = (bu["xblk"], bu["ln_bf"], bu["tsc"],
                                            bu["tbf"], bu["wsl"], bu["bank"])
        xsrc = x_ap.rearrange("(dt p) t -> p dt t", p=128)
        tmin = tbf[:, 4 * TB:5 * TB]
        texp = tbf[:, 5 * TB:6 * TB]
        phik = tsc[:, 0:TB]
        vvv = tsc[:, TB:2 * TB]
        colk = tsc[:, 2 * TB:2 * TB + 1]
        colv = tsc[:, 2 * TB + 1:2 * TB + 2]

        prev_ln = None
        l1_gate = None
        bank_gate = [None, None]
        prev_phi = None
        for s in range(NBLK):
            par = s % 2
            xv = xblk[:, par * DT * TB:(par + 1) * DT * TB]
            lv = ln_bf[:, par * DT * TB:(par + 1) * DT * TB]
            xld = p.emit("sync", lambda e2, s=s, par=par: e2.dma_start(
                out=xblk[:, par * DT * TB:(par + 1) * DT * TB].rearrange(
                    "p (dt t) -> p dt t", t=TB),
                in_=xsrc[:, :, s * TB:(s + 1) * TB]),
                waits=[prev_ln], inc="XL", inc_by=16)
            ln_tok = emit_ln(p, bu, xv, lv, xld, lv_gate=l1_gate)
            if lnx_ap is not None:
                ln_tok = p.emit("gpsimd", lambda e2, s=s, par=par: e2.dma_start(
                    out=ln_bf[:, par * DT * TB:(par + 1) * DT * TB].rearrange(
                        "p (dt t) -> p dt t", t=TB),
                    in_=lnx_ap.rearrange("(dt p) t -> p dt t", p=128)[:, :, s * TB:(s + 1) * TB]),
                    waits=[ln_tok], inc="XL", inc_by=16)
            prev_ln = ln_tok

            slab_free = [l1_gate, l1_gate]
            for dt in range(DT):
                r = dt % 2
                sl = wsl[r]
                p.emit("gpsimd", lambda e2, sl=sl, dt=dt: e2.dma_start(
                    out=sl[:, 0:2048], in_=wkv_ap[dt]),
                    waits=[slab_free[r]], inc="WL", inc_by=16)
                sr = p.emit("gpsimd", lambda e2, sl=sl, dt=dt: e2.dma_start(
                    out=sl[:, 2048:4096], in_=wkv_ap[16 + dt]), inc="WL", inc_by=16)
                pi = dt % 2
                b0 = 2 * pi
                lastmm = None
                for kt in range(DT):
                    p.emit("tensor", lambda e2, sl=sl, kt=kt, b0=b0, lv=lv: e2.matmul(
                        bank[b0][:, :], sl[:, kt * 128:(kt + 1) * 128],
                        lv[:, kt * TB:(kt + 1) * TB],
                        start=(kt == 0), stop=(kt == DT - 1)),
                        waits=[sr, ln_tok, bank_gate[pi] if kt == 0 else None],
                        inc="PE")
                    lastmm = p.emit("tensor", lambda e2, sl=sl, kt=kt, b0=b0, lv=lv: e2.matmul(
                        bank[b0 + 1][:, :],
                        sl[:, 2048 + kt * 128: 2048 + (kt + 1) * 128],
                        lv[:, kt * TB:(kt + 1) * TB],
                        start=(kt == 0), stop=(kt == DT - 1)), inc="PE")
                slab_free[r] = lastmm
                t1 = p.emit("vector", lambda e2, b0=b0: e2.tensor_scalar_min(
                    tmin, bank[b0][:, :], 0.0), waits=[lastmm, prev_phi], inc="VE")
                t2 = p.emit("scalar", lambda e2: e2.activation(
                    texp, tmin, AF.Exp), waits=[t1, prev_phi], inc="SE")
                t3 = p.emit("vector", lambda e2, b0=b0: e2.scalar_tensor_tensor(
                    phik, bank[b0][:, :], 0.0, texp, ALU.max, ALU.add),
                    waits=[t2], inc="VE")
                prev_phi = t3
                p.emit("vector", lambda e2, b0=b0: e2.tensor_copy(
                    vvv, bank[b0 + 1][:, :]), waits=[lastmm], inc="VE")
                rk = p.emit("vector", lambda e2: e2.reduce_sum(
                    colk, phik, axis=AX.X), inc="V2")
                p.emit("vector", lambda e2: e2.tensor_mul(vvv, phik, vvv), inc="VE")
                rv = p.emit("vector", lambda e2: e2.reduce_sum(
                    colv, vvv, axis=AX.X), inc="V2")
                p.emit("vector", lambda e2, dt=dt: e2.tensor_add(
                    bu["ks"][:, dt:dt + 1], bu["ks"][:, dt:dt + 1], colk),
                    waits=[rk], inc="VE")
                bank_gate[pi] = p.emit("vector", lambda e2, dt=dt: e2.tensor_add(
                    bu["ks"][:, 16 + dt:17 + dt], bu["ks"][:, 16 + dt:17 + dt],
                    colv), waits=[rv], inc="VE")
            l1_gate = lastmm

        st = p.emit("scalar", lambda e2: e2.dma_start(
            out=sums_ap, in_=bu["ks"][:, :]), waits=[p.tok("VE")],
            inc="ST", inc_by=16)
        p.emit("sync", None, waits=[st])
        p.finalize(stack)
    return nc



def build_l2():
    nc = bass.Bass("TRN2", target_bir_lowering=False, debug=False)
    ins = {}
    for nm, shape, dt_ in (
            ("x", [D, CHUNK], F32), ("wqkv", [48, 128, 2048], F32),
            ("wout", [16, 128, 2048], F32), ("wg", [64, 128, 2048], I8),
            ("wu", [64, 128, 2048], I8), ("wd", [32, 128, 4096], I8),
            ("sg", [128, 64], F32), ("su", [128, 64], F32), ("sd", [128, 16], F32),
            ("xp", [128, DT * 96], F32), ("dtp", [DTR, D], F32),
            ("dtb", [128, 16], F32), ("alog", [128, 256], F32),
            ("dv", [128, 16], F32), ("koff", [3, 128, 32], F32)):
        ins[nm] = nc.dram_tensor(nm, shape, dt_, kind="ExternalInput").ap()
    outs = {}
    for nm, shape, dt_ in (("xloc", [D, CHUNK], F32), ("scum", [D, CHUNK], F32),
                           ("cexp", [32, CHUNK], BF16), ("eexp", [128, 256], F32),
                           ("dsum", [128, 16], F32)):
        outs[nm] = nc.dram_tensor(nm, shape, dt_, kind="ExternalOutput").ap()
    import os
    if os.environ.get("L2_DBG"):
        outs["x1exp"] = nc.dram_tensor("x1exp", [D, CHUNK], F32,
                                       kind="ExternalOutput").ap()
        outs["x2exp"] = nc.dram_tensor("x2exp", [D, CHUNK], F32,
                                       kind="ExternalOutput").ap()
        outs["ln2exp"] = nc.dram_tensor("ln2exp", [D, CHUNK], BF16,
                                        kind="ExternalOutput").ap()

    with ExitStack() as stack:
        bu = alloc_common(nc, stack, "l2")
        e = stack.enter_context
        for nm, sh, dt_ in (("sg_c", [128, 64], F32), ("su_c", [128, 64], F32),
                            ("sd_c", [128, 16], F32), ("dtb_c", [128, 16], F32),
                            ("dv_c", [128, 16], F32), ("napos", [128, 256], F32),
                            ("kvcar", [128, 32], F32), ("scar", [128, 16], F32),
                            ("stcar", [128, 256], F32), ("koi", [128, 96], F32),
                            ("xps", [128, DT * 96], BF16), ("dtps", [64, D], BF16),
                            ("xdl", [64, TB], BF16), ("bcr", [32, TB], BF16)):
            bu[nm] = e(nc.sbuf_tensor(nm, sh, dt_))

        p = Prog(nc)
        emit_consts(p, bu)
        for nm in ("stcar", "scar"):
            p.emit("vector", lambda e2, nm=nm: e2.memset(bu[nm][:, :], 0.0), inc="VE")
        for src, dst in (("sg", "sg_c"), ("su", "su_c"), ("sd", "sd_c"),
                         ("dtb", "dtb_c"), ("dv", "dv_c"), ("alog", "napos")):
            p.emit("sync", lambda e2, src=src, dst=dst: e2.dma_start(
                out=bu[dst][:, :], in_=ins[src]), inc="XL", inc_by=16)
        p.emit("sync", lambda e2: e2.dma_start(
            out=bu["koi"][:, :].rearrange("p (s c) -> p s c", c=32),
            in_=ins["koff"].rearrange("s p c -> p s c")), inc="XL", inc_by=16)
        p.emit("gpsimd", lambda e2: e2.dma_start(
            out=bu["xps"][:, :], in_=ins["xp"]), inc="WL", inc_by=16)
        p.emit("gpsimd", lambda e2: e2.dma_start(
            out=bu["dtps"][:, :], in_=ins["dtp"]), inc="WL", inc_by=16)
        ld = p.tok("XL")
        s1 = p.emit("scalar", lambda e2: e2.activation(
            bu["napos"][:, :], bu["napos"][:, :], AF.Exp), waits=[ld], inc="SE")
        p.emit("vector", lambda e2: e2.tensor_scalar_mul(
            bu["napos"][:, :], bu["napos"][:, :], -1.0), waits=[s1], inc="VE")
        p.emit("vector", lambda e2: e2.tensor_add(
            bu["kvcar"][:, :], bu["koi"][:, 0:32], bu["koi"][:, 32:64]),
            waits=[ld], inc="VE")
        p.emit("vector", lambda e2: e2.tensor_add(
            bu["kvcar"][:, :], bu["kvcar"][:, :], bu["koi"][:, 64:96]), inc="VE")
        p.barrier()

        xblk, ln_bf, ubuf, tsc, tbf = (bu["xblk"], bu["ln_bf"], bu["ubuf"],
                                       bu["tsc"], bu["tbf"])
        wsl, bank = bu["wsl"], bu["bank"]
        xsrc = ins["x"].rearrange("(dt p) t -> p dt t", p=128)
        xloc_dst = outs["xloc"].rearrange("(dt p) t -> p dt t", p=128)

        phiq = tsc[:, 0:TB]
        phik = tsc[:, TB:2 * TB]
        vkv = tsc[:, 2 * TB:3 * TB]
        kcd = tsc[:, 3 * TB:4 * TB]
        kvn = tsc[:, 4 * TB:5 * TB]
        tmin = tbf[:, 0:TB]
        texp = tbf[:, TB:2 * TB]
        dA = [tbf[:, 4 * TB:5 * TB], tbf[:, 5 * TB:6 * TB]]
        dBu = tbf[:, 6 * TB:7 * TB]
        stt_ = tbf[:, 7 * TB:8 * TB]
        tmpy = tbf[:, 8 * TB:9 * TB]

        ps = {"slab_gate": None, "p2_ev": None, "ratio_readers": None,
              "delta_ve": None, "delta_se": None, "bc_exp": None}
        last_export = [None, None]

        def make_scan_filler(s_prev):
            par_p = s_prev % 2
            xvp = xblk[:, par_p * DT * TB:(par_p + 1) * DT * TB]
            uvp = ln_bf[:, par_p * DT * TB:(par_p + 1) * DT * TB]
            st = {"n": 0, "dt": 0, "prev_rd": [None, None],
                  "prev_scan": [None, None], "ld": [None, None],
                  "bc_exp": ps["bc_exp"], "dse": ps["delta_se"]}

            def one():
                if st["n"] >= NST:
                    return False
                n, dt = st["n"], st["dt"]
                q = n % 2
                if dt == 0:
                    p.emit("sync", lambda e2, n=n, q=q: e2.dma_start(
                        out=bu["bbc"][:, q * TB:(q + 1) * TB],
                        in_=outs["cexp"][n:n + 1,
                                         s_prev * TB:(s_prev + 1) * TB]
                        .broadcast_to([128, TB])),
                        waits=[st["bc_exp"], st["prev_rd"][q]],
                        inc="XL", inc_by=16)
                    st["ld"][q] = p.emit("sync", lambda e2, n=n, q=q: e2.dma_start(
                        out=bu["cbc"][:, q * TB:(q + 1) * TB],
                        in_=outs["cexp"][16 + n:17 + n,
                                         s_prev * TB:(s_prev + 1) * TB]
                        .broadcast_to([128, TB])), inc="XL", inc_by=16)
                col = dt * 16 + n
                r = dt % 2
                sa = p.emit("scalar", lambda e2, dt=dt, col=col, r=r: e2.activation(
                    dA[r], ubuf[:, 16384 + dt * TB: 16384 + (dt + 1) * TB], AF.Exp,
                    scale=bu["napos"][:, col:col + 1]),
                    waits=[st["prev_scan"][r], st["dse"]], inc="SE")
                p.emit("vector", lambda e2, dt=dt, q=q: e2.tensor_mul(
                    dBu, uvp[:, dt * TB:(dt + 1) * TB],
                    bu["bbc"][:, q * TB:(q + 1) * TB]),
                    waits=[st["ld"][q]], inc="VE")
                sc = p.emit("vector", lambda e2, col=col, r=r: e2.tensor_tensor_scan(
                    stt_, dA[r], dBu, bu["stcar"][:, col:col + 1],
                    ALU.mult, ALU.add), waits=[sa], inc="V2")
                st["prev_scan"][r] = sc
                p.emit("vector", lambda e2, col=col: e2.tensor_copy(
                    bu["stcar"][:, col:col + 1], stt_[:, TB - 1:TB]),
                    waits=[sc], inc="VE")
                rd = p.emit("vector", lambda e2, q=q: e2.tensor_mul(
                    tmpy, stt_, bu["cbc"][:, q * TB:(q + 1) * TB]),
                    waits=[sc], inc="VE")
                st["prev_rd"][q] = rd
                p.emit("vector", lambda e2, dt=dt: e2.tensor_add(
                    xvp[:, dt * TB:(dt + 1) * TB],
                    xvp[:, dt * TB:(dt + 1) * TB], tmpy), inc="VE")
                st["dt"] += 1
                if st["dt"] == DT:
                    st["dt"] = 0
                    st["n"] += 1
                return True

            def drain():
                while one():
                    pass
            return one, drain

        filler = drain = None

        for s in range(NBLK):
            par = s % 2
            xv = xblk[:, par * DT * TB:(par + 1) * DT * TB]
            lv = ln_bf[:, par * DT * TB:(par + 1) * DT * TB]
            xld = p.emit("sync", lambda e2, s=s, par=par: e2.dma_start(
                out=xblk[:, par * DT * TB:(par + 1) * DT * TB].rearrange(
                    "p (dt t) -> p dt t", t=TB),
                in_=xsrc[:, :, s * TB:(s + 1) * TB]),
                waits=[last_export[par]], inc="XL", inc_by=16)

            ln_tok = emit_ln(p, bu, xv, lv, xld, lv_gate=ps["slab_gate"])
            fuse_gate = ps["slab_gate"]
            slab_free = [ps["slab_gate"], ps["slab_gate"]]
            bank_gate = [ps["delta_ve"], ps["delta_ve"]]
            prev_chain = None
            for dt in range(DT):
                r = dt % 2
                sl = wsl[r]
                sr = None
                for j in range(3):
                    sr = p.emit("gpsimd", lambda e2, sl=sl, j=j, dt=dt: e2.dma_start(
                        out=sl[:, j * 2048:(j + 1) * 2048],
                        in_=ins["wqkv"][j * 16 + dt]),
                        waits=[slab_free[r]] if j == 0 else (), inc="WL", inc_by=16)
                pi = dt % 2
                b0 = 3 * pi
                gate = bank_gate[pi]
                lastmm = None
                for j in range(3):
                    for kt in range(DT):
                        lastmm = p.emit("tensor", lambda e2, sl=sl, j=j, kt=kt, b0=b0, lv=lv: e2.matmul(
                            bank[b0 + j][:, :],
                            sl[:, j * 2048 + kt * 128: j * 2048 + (kt + 1) * 128],
                            lv[:, kt * TB:(kt + 1) * TB],
                            start=(kt == 0), stop=(kt == DT - 1)),
                            waits=[sr, ln_tok, gate if kt == 0 else None], inc="PE")
                slab_free[r] = lastmm
                ps["slab_gate"] = lastmm
                for j, dst in ((0, phiq), (1, phik)):
                    t1 = p.emit("vector", lambda e2, j=j, b0=b0: e2.tensor_scalar_min(
                        tmin, bank[b0 + j][:, :], 0.0),
                        waits=[lastmm, prev_chain], inc="VE")
                    t2 = p.emit("scalar", lambda e2: e2.activation(
                        texp, tmin, AF.Exp), waits=[t1, prev_chain], inc="SE")
                    t3 = p.emit("vector", lambda e2, j=j, b0=b0, dst=dst: e2.scalar_tensor_tensor(
                        dst, bank[b0 + j][:, :], 0.0, texp, ALU.max, ALU.add),
                        waits=[t2], inc="VE")
                    prev_chain = t3
                p.emit("vector", lambda e2, b0=b0: e2.tensor_copy(
                    vkv, bank[b0 + 2][:, :]), waits=[lastmm], inc="VE")
                p.emit("vector", lambda e2: e2.tensor_mul(vkv, phik, vkv), inc="VE")
                sck = p.emit("vector", lambda e2, dt=dt: e2.tensor_tensor_scan(
                    kcd, bu["onest"][:, :], phik, bu["kvcar"][:, dt:dt + 1],
                    ALU.mult, ALU.add), inc="V2")
                sckv = p.emit("vector", lambda e2, dt=dt: e2.tensor_tensor_scan(
                    kvn, bu["onest"][:, :], vkv, bu["kvcar"][:, 16 + dt:17 + dt],
                    ALU.mult, ALU.add), inc="V2")
                p.emit("vector", lambda e2, dt=dt: e2.tensor_copy(
                    bu["kvcar"][:, dt:dt + 1], kcd[:, TB - 1:TB]),
                    waits=[sck], inc="VE")
                p.emit("vector", lambda e2, dt=dt: e2.tensor_copy(
                    bu["kvcar"][:, 16 + dt:17 + dt], kvn[:, TB - 1:TB]),
                    waits=[sckv], inc="VE")
                p.emit("vector", lambda e2: e2.tensor_mul(kvn, phiq, kvn),
                       waits=[sckv], inc="VE")
                dtok = p.emit("vector", lambda e2: e2.tensor_mul(kcd, phiq, kcd),
                              waits=[sck], inc="VE")
                sr1 = p.emit("scalar", lambda e2: e2.activation(
                    kcd, kcd, AF.Ln, bias=bu["cst"][:, 4:5]), waits=[dtok], inc="SE")
                sr2 = p.emit("scalar", lambda e2: e2.activation(
                    kcd, kcd, AF.Exp, scale=bu["cst"][:, 5:6]), waits=[sr1], inc="SE")
                prev_chain = p.emit("vector", lambda e2, dt=dt: e2.tensor_mul(
                    ubuf[:, dt * TB:(dt + 1) * TB], kvn, kcd),
                    waits=[sr2, fuse_gate if dt == 0 else None], inc="VE")
                bank_gate[pi] = prev_chain
                if filler is not None and INTERLEAVE:
                    filler()
                    filler()
            attn_ev = p.tok("VE")

            slab_free = [ps["slab_gate"], ps["slab_gate"]]
            prev_ev = [None] * 4
            wout_last = None
            for m in range(DT):
                r = m % 2
                sl = wsl[r]
                sr = p.emit("gpsimd", lambda e2, sl=sl, m=m: e2.dma_start(
                    out=sl[:, 0:2048], in_=ins["wout"][m]),
                    waits=[slab_free[r]], inc="WL", inc_by=16)
                bk = m % 4
                lastmm = None
                for kt in range(DT):
                    lastmm = p.emit("tensor", lambda e2, sl=sl, kt=kt, bk=bk: e2.matmul(
                        bank[bk][:, :], sl[:, kt * 128:(kt + 1) * 128],
                        ubuf[:, kt * TB:(kt + 1) * TB],
                        start=(kt == 0), stop=(kt == DT - 1)),
                        waits=[sr, attn_ev, prev_ev[bk]], inc="PE")
                slab_free[r] = lastmm
                ps["slab_gate"] = lastmm
                wout_last = lastmm
                prev_ev[bk] = p.emit("vector", lambda e2, m=m, bk=bk, xv=xv: e2.scalar_tensor_tensor(
                    xv[:, m * TB:(m + 1) * TB], bank[bk][:, :], HD_SCALE,
                    xv[:, m * TB:(m + 1) * TB], ALU.mult, ALU.add),
                    waits=[lastmm], inc="VE")
                if filler is not None and INTERLEAVE:
                    filler()
            ps["ratio_readers"] = wout_last
            if "x1exp" in outs:
                p.emit("scalar", lambda e2, s=s, par=par: e2.dma_start(
                    out=outs["x1exp"].rearrange("(dt p) t -> p dt t", p=128)[
                        :, :, s * TB:(s + 1) * TB],
                    in_=xblk[:, par * DT * TB:(par + 1) * DT * TB]
                    .rearrange("p (dt t) -> p dt t", t=TB)),
                    waits=[p.tok("VE")], inc="ST", inc_by=16)

            ln_tok = emit_ln(p, bu, xv, lv, None, lv_gate=ps["slab_gate"])
            ln2_tok = None
            if "ln2exp" in outs:
                ln2_tok = p.emit("scalar", lambda e2, s=s, par=par: e2.dma_start(
                    out=outs["ln2exp"].rearrange("(dt p) t -> p dt t", p=128)[
                        :, :, s * TB:(s + 1) * TB],
                    in_=ln_bf[:, par * DT * TB:(par + 1) * DT * TB]
                    .rearrange("p (dt t) -> p dt t", t=TB)),
                    waits=[ln_tok], inc="ST", inc_by=16)
            emit_ffn(p, bu, ins, xv, lv, ln_tok, ps,
                     filler=filler if INTERLEAVE else None)
            if drain is not None:
                drain()
                par_p = (s - 1) % 2
                last_export[par_p] = p.emit(
                    "scalar", lambda e2, sp=s - 1, par_p=par_p: e2.dma_start(
                        out=xloc_dst[:, :, sp * TB:(sp + 1) * TB],
                        in_=xblk[:, par_p * DT * TB:(par_p + 1) * DT * TB]
                        .rearrange("p (dt t) -> p dt t", t=TB)),
                    waits=[p.tok("VE")], inc="ST", inc_by=16)
                filler = drain = None

            if "x2exp" in outs:
                p.emit("scalar", lambda e2, s=s, par=par: e2.dma_start(
                    out=outs["x2exp"].rearrange("(dt p) t -> p dt t", p=128)[
                        :, :, s * TB:(s + 1) * TB],
                    in_=xblk[:, par * DT * TB:(par + 1) * DT * TB]
                    .rearrange("p (dt t) -> p dt t", t=TB)),
                    waits=[ps["p2_ev"]], inc="ST", inc_by=16)
            ln_tok = emit_ln(p, bu, xv, lv, None,
                             lv_gate=ln2_tok if ln2_tok is not None
                             else ps["slab_gate"])
            lastmm = None
            for kt in range(DT):
                lastmm = p.emit("tensor", lambda e2, kt=kt, lv=lv: e2.matmul(
                    bank[6][0:96, :], bu["xps"][:, kt * 96:(kt + 1) * 96],
                    lv[:, kt * TB:(kt + 1) * TB],
                    start=(kt == 0), stop=(kt == DT - 1)),
                    waits=[ln_tok], inc="PE")
            p.emit("vector", lambda e2: e2.tensor_copy(
                bu["xdl"][:, :], bank[6][0:64, :]), waits=[lastmm], inc="VE")
            p.emit("vector", lambda e2: e2.tensor_copy(
                bu["bcr"][:, :], bank[6][64:96, :]), waits=[lastmm], inc="VE")
            xdl_tok = p.tok("VE")
            ps["bc_exp"] = p.emit("scalar", lambda e2, s=s: e2.dma_start(
                out=outs["cexp"][:, s * TB:(s + 1) * TB],
                in_=bu["bcr"][0:32, :]), waits=[xdl_tok], inc="ST", inc_by=16)

            zt = tsc[:, 0:TB]
            pt = tsc[:, TB:2 * TB]
            scm = tsc[:, 2 * TB:3 * TB]
            scm_exp = None
            dl_gate = [None, None]
            for m in range(DT):
                mm1 = p.emit("tensor", lambda e2, m=m: e2.matmul(
                    bank[m % 2][:, :], bu["dtps"][0:64, m * 128:(m + 1) * 128],
                    bu["xdl"][:, :], start=True, stop=True),
                    waits=[xdl_tok, dl_gate[m % 2]], inc="PE")
                t1 = p.emit("vector", lambda e2, m=m: e2.tensor_scalar(
                    zt, bank[m % 2][:, :], bu["dtb_c"][:, m:m + 1], 30.0,
                    ALU.add, ALU.min), waits=[mm1], inc="VE")
                dl_gate[m % 2] = t1
                t2 = p.emit("scalar", lambda e2: e2.activation(
                    pt, zt, AF.Exp), waits=[t1], inc="SE")
                t3 = p.emit("scalar", lambda e2, m=m: e2.activation(
                    ubuf[:, 16384 + m * TB: 16384 + (m + 1) * TB], pt, AF.Ln,
                    bias=bu["cst"][:, 0:1]), waits=[t2], inc="SE")
                p.emit("vector", lambda e2, m=m, xv=xv, lv=lv: e2.scalar_tensor_tensor(
                    xv[:, m * TB:(m + 1) * TB], lv[:, m * TB:(m + 1) * TB],
                    bu["dv_c"][:, m:m + 1], xv[:, m * TB:(m + 1) * TB],
                    ALU.mult, ALU.add), waits=[t3], inc="VE")
                sc = p.emit("vector", lambda e2, m=m: e2.tensor_tensor_scan(
                    scm, bu["onest"][:, :],
                    ubuf[:, 16384 + m * TB: 16384 + (m + 1) * TB],
                    bu["scar"][:, m:m + 1], ALU.mult, ALU.add),
                    waits=[scm_exp], inc="V2")
                p.emit("vector", lambda e2, m=m: e2.tensor_copy(
                    bu["scar"][:, m:m + 1], scm[:, TB - 1:TB]),
                    waits=[sc], inc="VE")
                scm_exp = p.emit("scalar", lambda e2, m=m, s=s: e2.dma_start(
                    out=outs["scum"][m * 128:(m + 1) * 128, s * TB:(s + 1) * TB],
                    in_=scm[:, :]), waits=[sc], inc="ST", inc_by=16)
                p.emit("vector", lambda e2, m=m, lv=lv: e2.tensor_mul(
                    lv[:, m * TB:(m + 1) * TB],
                    ubuf[:, 16384 + m * TB: 16384 + (m + 1) * TB],
                    lv[:, m * TB:(m + 1) * TB]), inc="VE")
            ps["delta_ve"] = p.tok("VE")
            ps["delta_se"] = p.tok("SE")

            filler, drain = make_scan_filler(s)

        drain()
        par_p = (NBLK - 1) % 2
        p.emit("scalar", lambda e2: e2.dma_start(
            out=xloc_dst[:, :, (NBLK - 1) * TB:NBLK * TB],
            in_=xblk[:, par_p * DT * TB:(par_p + 1) * DT * TB]
            .rearrange("p (dt t) -> p dt t", t=TB)),
            waits=[p.tok("VE")], inc="ST", inc_by=16)
        p.emit("scalar", lambda e2: e2.dma_start(
            out=outs["eexp"], in_=bu["stcar"][:, :]),
            waits=[p.tok("VE")], inc="ST", inc_by=16)
        p.emit("scalar", lambda e2: e2.dma_start(
            out=outs["dsum"], in_=bu["scar"][:, :]), inc="ST", inc_by=16)
        p.emit("sync", None, waits=[p.tok("ST")])
        p.finalize(stack)
    return nc



def build_l3():
    nc = bass.Bass("TRN2", target_bir_lowering=False, debug=False)
    ins = {}
    for nm, shape, dt_ in (
            ("xloc", [D, CHUNK], F32), ("scum", [D, CHUNK], F32),
            ("cexp", [32, CHUNK], BF16), ("epred", [3, 128, 256], F32),
            ("dsr", [3, 128, 256], F32), ("alog", [128, 256], F32),
            ("wg", [64, 128, 2048], I8), ("wu", [64, 128, 2048], I8),
            ("wd", [32, 128, 4096], I8),
            ("sg", [128, 64], F32), ("su", [128, 64], F32), ("sd", [128, 16], F32)):
        ins[nm] = nc.dram_tensor(nm, shape, dt_, kind="ExternalInput").ap()
    y_ap = nc.dram_tensor("y", [D, CHUNK], F32, kind="ExternalOutput").ap()

    with ExitStack() as stack:
        bu = alloc_common(nc, stack, "l3")
        e = stack.enter_context
        for nm, sh in (("sg_c", [128, 64]), ("su_c", [128, 64]), ("sd_c", [128, 16]),
                       ("napos", [128, 256]), ("s0", [128, 256])):
            bu[nm] = e(nc.sbuf_tensor(nm, sh, F32))
        bu["eds"] = bu["tsc"][:, 0:1536]

        p = Prog(nc)
        emit_consts(p, bu)
        for src, dst in (("sg", "sg_c"), ("su", "su_c"), ("sd", "sd_c"),
                         ("alog", "napos")):
            p.emit("sync", lambda e2, src=src, dst=dst: e2.dma_start(
                out=bu[dst][:, :], in_=ins[src]), inc="XL", inc_by=16)
        for s in range(3):
            p.emit("sync", lambda e2, s=s: e2.dma_start(
                out=bu["eds"][:, s * 256:(s + 1) * 256], in_=ins["epred"][s]),
                inc="XL", inc_by=16)
            p.emit("sync", lambda e2, s=s: e2.dma_start(
                out=bu["eds"][:, (3 + s) * 256:(4 + s) * 256], in_=ins["dsr"][s]),
                inc="XL", inc_by=16)
        ld = p.tok("XL")
        s1 = p.emit("scalar", lambda e2: e2.activation(
            bu["napos"][:, :], bu["napos"][:, :], AF.Exp), waits=[ld], inc="SE")
        p.emit("vector", lambda e2: e2.tensor_scalar_mul(
            bu["napos"][:, :], bu["napos"][:, :], -1.0), waits=[s1], inc="VE")
        p.emit("vector", lambda e2: e2.memset(bu["s0"][:, :], 0.0), inc="VE")
        tA = bu["tsc"][:, 1536:1792]
        for s in range(3):
            v1 = p.emit("vector", lambda e2, s=s: e2.tensor_mul(
                tA, bu["napos"][:, :], bu["eds"][:, (3 + s) * 256:(4 + s) * 256]),
                waits=[ld], inc="VE")
            s2 = p.emit("scalar", lambda e2: e2.activation(
                tA, tA, AF.Exp), waits=[v1], inc="SE")
            p.emit("vector", lambda e2: e2.tensor_mul(
                bu["s0"][:, :], bu["s0"][:, :], tA), waits=[s2], inc="VE")
            p.emit("vector", lambda e2, s=s: e2.tensor_add(
                bu["s0"][:, :], bu["s0"][:, :], bu["eds"][:, s * 256:(s + 1) * 256]),
                inc="VE")
        p.barrier()

        xblk, ln_bf, ubuf, tsc, tbf = (bu["xblk"], bu["ln_bf"], bu["ubuf"],
                                       bu["tsc"], bu["tbf"])
        bank = bu["bank"]
        xsrc = ins["xloc"].rearrange("(dt p) t -> p dt t", p=128)
        y_dst = y_ap.rearrange("(dt p) t -> p dt t", p=128)

        scm = [tsc[:, 0:TB], tsc[:, TB:2 * TB]]
        cp = [tbf[:, 4 * TB:5 * TB], tbf[:, 5 * TB:6 * TB]]
        tt = tbf[:, 6 * TB:7 * TB]

        ps = {"slab_gate": None, "p2_ev": None, "ratio_readers": None,
              "ln_gate": None}
        last_export = [None, None]

        def make_corr_filler(s, xld_tok):
            par = s % 2
            xv = xblk[:, par * DT * TB:(par + 1) * DT * TB]
            cvw = ubuf[:, 16384 + par * 8192: 16384 + par * 8192 + 8192]
            cre = p.tok("VE")
            st = {"dt": 0, "n": 0, "sld": [None, None],
                  "prev_v1": [None, None], "cld": None, "xld": xld_tok}

            def one():
                if st["dt"] >= DT:
                    return False
                dt, n = st["dt"], st["n"]
                r = dt % 2
                if dt == 0 and n == 0:
                    for nn in range(NST):
                        st["cld"] = p.emit("sync", lambda e2, nn=nn, s=s: e2.dma_start(
                            out=cvw[:, nn * TB:(nn + 1) * TB],
                            in_=ins["cexp"][16 + nn:17 + nn,
                                            s * TB:(s + 1) * TB]
                            .broadcast_to([128, TB])),
                            waits=[cre] if nn == 0 else (), inc="XL", inc_by=16)
                if n == 0:
                    st["sld"][r] = p.emit("sync", lambda e2, dt=dt, s=s, r=r: e2.dma_start(
                        out=scm[r][:, :],
                        in_=ins["scum"][dt * 128:(dt + 1) * 128,
                                        s * TB:(s + 1) * TB]),
                        waits=[st["prev_v1"][0], st["prev_v1"][1]]
                        if dt >= 2 else [ps["ln_gate"]],
                        inc="XL", inc_by=16)
                col = dt * 16 + n
                q = n % 2
                sx1 = p.emit("scalar", lambda e2, r=r, col=col, q=q: e2.activation(
                    cp[q][:, :], scm[r][:, :], AF.Exp,
                    scale=bu["napos"][:, col:col + 1]),
                    waits=[st["sld"][r], st["prev_v1"][q], st["cld"]], inc="SE")
                v1 = p.emit("vector", lambda e2, n=n, q=q, col=col: e2.scalar_tensor_tensor(
                    tt, cp[q][:, :], bu["s0"][:, col:col + 1],
                    cvw[:, n * TB:(n + 1) * TB],
                    ALU.mult, ALU.mult), waits=[sx1], inc="VE")
                st["prev_v1"][q] = v1
                p.emit("vector", lambda e2, dt=dt: e2.tensor_add(
                    xv[:, dt * TB:(dt + 1) * TB],
                    xv[:, dt * TB:(dt + 1) * TB], tt),
                    waits=[st["xld"]], inc="VE")
                st["n"] += 1
                if st["n"] == NST:
                    st["n"] = 0
                    st["dt"] += 1
                return True

            def drain():
                while one():
                    pass
            return one, drain

        filler = drain = None
        prev_corr_ve = None

        for s in range(NBLK):
            par = s % 2
            xld = p.emit("sync", lambda e2, s=s, par=par: e2.dma_start(
                out=xblk[:, par * DT * TB:(par + 1) * DT * TB].rearrange(
                    "p (dt t) -> p dt t", t=TB),
                in_=xsrc[:, :, s * TB:(s + 1) * TB]),
                waits=[last_export[par]], inc="XL", inc_by=16)
            filler, drain = make_corr_filler(s, xld)

            if s == 0:
                drain()
                filler = drain = None
                prev_corr_ve = p.tok("VE")
            else:
                sp = s - 1
                par_p = sp % 2
                xvp = xblk[:, par_p * DT * TB:(par_p + 1) * DT * TB]
                lvp = ln_bf[:, par_p * DT * TB:(par_p + 1) * DT * TB]
                ln_tok = emit_ln(p, bu, xvp, lvp, prev_corr_ve,
                                 lv_gate=ps["slab_gate"])
                ps["ln_gate"] = ln_tok
                emit_ffn(p, bu, ins, xvp, lvp, ln_tok, ps,
                         filler=filler if INTERLEAVE else None)
                if drain is not None:
                    drain()
                    filler = drain = None
                prev_corr_ve = p.tok("VE")
                last_export[par_p] = p.emit(
                    "scalar", lambda e2, sp=sp, par_p=par_p: e2.dma_start(
                        out=y_dst[:, :, sp * TB:(sp + 1) * TB],
                        in_=xblk[:, par_p * DT * TB:(par_p + 1) * DT * TB]
                        .rearrange("p (dt t) -> p dt t", t=TB)),
                    waits=[ps["p2_ev"]], inc="ST", inc_by=16)

        sp = NBLK - 1
        par_p = sp % 2
        xvp = xblk[:, par_p * DT * TB:(par_p + 1) * DT * TB]
        lvp = ln_bf[:, par_p * DT * TB:(par_p + 1) * DT * TB]
        ln_tok = emit_ln(p, bu, xvp, lvp, prev_corr_ve,
                         lv_gate=ps["slab_gate"])
        ps["ln_gate"] = ln_tok
        emit_ffn(p, bu, ins, xvp, lvp, ln_tok, ps, filler=None)
        p.emit("scalar", lambda e2, sp=sp, par_p=par_p: e2.dma_start(
            out=y_dst[:, :, sp * TB:(sp + 1) * TB],
            in_=xblk[:, par_p * DT * TB:(par_p + 1) * DT * TB]
            .rearrange("p (dt t) -> p dt t", t=TB)),
            waits=[ps["p2_ev"]], inc="ST", inc_by=16)
        p.emit("sync", None, waits=[p.tok("ST")])
        p.finalize(stack)
    return nc



def _col128(v):
    return np.ascontiguousarray(np.asarray(v, np.float32).reshape(-1, 128).T)


def kernel(L0_attn_a_x, L0_attn_a_W_qkv, L0_attn_a_W_out,
           L0_attn_b_x_proj_weight, L0_attn_b_dt_proj_weight,
           L0_attn_b_dt_proj_bias, L0_attn_b_A_log, L0_attn_b_D,
           L0_ffn_w_gate, L0_ffn_w_up, L0_ffn_w_down,
           L0_ffn_s_gate, L0_ffn_s_up, L0_ffn_s_down):
    f32 = np.float32
    x = np.asarray(L0_attn_a_x, f32)
    x_cm = np.ascontiguousarray(x.transpose(0, 2, 1))
    wqkv_t = np.asarray(L0_attn_a_W_qkv, f32).T
    wqkv = np.ascontiguousarray(
        wqkv_t.reshape(16, 128, 48, 128).transpose(2, 1, 0, 3).reshape(48, 128, 2048))
    wkv = np.ascontiguousarray(wqkv[16:])
    wout = np.ascontiguousarray(
        np.asarray(L0_attn_a_W_out, f32).T
        .reshape(16, 128, 16, 128).transpose(2, 1, 0, 3).reshape(16, 128, 2048))
    wg = np.ascontiguousarray(np.asarray(L0_ffn_w_gate)
                              .reshape(16, 128, 64, 128).transpose(2, 1, 0, 3)
                              .reshape(64, 128, 2048))
    wu = np.ascontiguousarray(np.asarray(L0_ffn_w_up)
                              .reshape(16, 128, 64, 128).transpose(2, 1, 0, 3)
                              .reshape(64, 128, 2048))
    wd = np.ascontiguousarray(np.asarray(L0_ffn_w_down)
                              .reshape(32, 2, 128, 2048).transpose(0, 2, 1, 3)
                              .reshape(32, 128, 4096))
    sg_c, su_c, sd_c = _col128(L0_ffn_s_gate), _col128(L0_ffn_s_up), _col128(L0_ffn_s_down)
    dtb_c, dv_c = _col128(L0_attn_b_dt_proj_bias), _col128(L0_attn_b_D)
    alog_cm = np.ascontiguousarray(
        np.asarray(L0_attn_b_A_log, f32).reshape(16, 128, 16)
        .transpose(1, 0, 2).reshape(128, 256))
    xp = np.ascontiguousarray(np.asarray(L0_attn_b_x_proj_weight, f32)
                              .reshape(16, 128, 96).transpose(1, 0, 2).reshape(128, 1536))
    dtp = np.ascontiguousarray(np.asarray(L0_attn_b_dt_proj_weight, f32))

    xs = [np.ascontiguousarray(x_cm[c // 4][:, (c % 4) * CHUNK:(c % 4 + 1) * CHUNK])
          for c in range(NCORE)]

    nc1 = build_l1()
    res1 = run_bass_kernel_spmd(
        nc1, [{"x": xs[c], "wkv": wkv} for c in range(NCORE)],
        core_ids=list(range(NCORE)))
    sums = [np.asarray(res1.results[c]["sums"]) for c in range(NCORE)]

    nc2 = build_l2()
    maps2 = []
    for c in range(NCORE):
        bi, j = c // 4, c % 4
        koff = np.zeros((3, 128, 32), f32)
        for i in range(j):
            koff[i] = sums[bi * 4 + i]
        maps2.append({"x": xs[c], "wqkv": wqkv, "wout": wout,
                      "wg": wg, "wu": wu, "wd": wd, "sg": sg_c, "su": su_c,
                      "sd": sd_c, "xp": xp, "dtp": dtp, "dtb": dtb_c,
                      "alog": alog_cm, "dv": dv_c, "koff": koff})
    res2 = run_bass_kernel_spmd(nc2, maps2, core_ids=list(range(NCORE)))

    nc3 = build_l3()
    maps3 = []
    for c in range(NCORE):
        bi, j = c // 4, c % 4
        epred = np.zeros((3, 128, 256), f32)
        dsr = np.zeros((3, 128, 256), f32)
        for i in range(j):
            rc = res2.results[bi * 4 + i]
            epred[i] = np.asarray(rc["eexp"])
            dsr[i] = np.repeat(np.asarray(rc["dsum"]), 16, axis=1)
        rme = res2.results[c]
        maps3.append({"xloc": np.asarray(rme["xloc"]),
                      "scum": np.asarray(rme["scum"]),
                      "cexp": np.asarray(rme["cexp"]),
                      "epred": epred, "dsr": dsr, "alog": alog_cm,
                      "wg": wg, "wu": wu, "wd": wd,
                      "sg": sg_c, "su": su_c, "sd": sd_c})
    res3 = run_bass_kernel_spmd(nc3, maps3, core_ids=list(range(NCORE)))

    out = np.empty((B, L, D), f32)
    for c in range(NCORE):
        bi, j = c // 4, c % 4
        out[bi, j * CHUNK:(j + 1) * CHUNK, :] = np.asarray(res3.results[c]["y"]).T
    return out
